# version 24
# speedup vs baseline: 1.1044x; 1.0080x over previous
import contextlib
import numpy as np
import ml_dtypes
import concourse.bass as bass
import concourse.mybir as mybir
from concourse.bass_utils import run_bass_kernel_spmd

F32 = mybir.dt.float32
BF16 = mybir.dt.bfloat16
I32 = mybir.dt.int32
AF = mybir.ActivationFunctionType
ALU = mybir.AluOpType
AX = mybir.AxisListType
RC_MAGIC = 12582912.0


def register_const(nc, value, ap):
    try:
        nc.const_aps.tensor(value, [128, 1], F32)
    except Exception:
        nc.const_aps.register(F32, value, ap)


class BlockWrap:

    def __init__(self, block, bar, base, counter):
        self._block, self._bar, self._base, self._counter = block, bar, base, counter

    def _wrap(self, dec):
        def take(fn):
            self._counter[0] += 1

            def wrapped(eng):
                if self._bar is not None:
                    eng.wait_ge(self._bar, self._base)
                fn(eng)
                if self._bar is not None:
                    eng.nop().then_inc(self._bar, 1)
            return dec(wrapped)
        return take

    @property
    def sync(self):
        return self._wrap(self._block.sync)

    @property
    def gpsimd(self):
        return self._wrap(self._block.gpsimd)

    @property
    def vector(self):
        return self._wrap(self._block.vector)

    @property
    def tensor(self):
        return self._wrap(self._block.tensor)

    @property
    def scalar(self):
        return self._wrap(self._block.scalar)


class Alloc:

    def __init__(self, nc, stack, semstack=None, bar=None, bar_base=0, counter=None):
        self.nc, self.stack = nc, stack
        self.semstack = semstack if semstack is not None else stack
        self.bar, self.bar_base = bar, bar_base
        self.counter = counter if counter is not None else [0]

    def sb(self, name, shape, dtype):
        return self.stack.enter_context(self.nc.sbuf_tensor(name, shape, dtype))

    def ps(self, name, shape, dtype):
        return self.stack.enter_context(self.nc.psum_tensor(name, shape, dtype))

    def dram(self, name, shape, dtype, **kw):
        return self.nc.dram_tensor(name, shape, dtype, kind="Internal", **kw).ap()

    def sem(self, name):
        return self.semstack.enter_context(self.nc.semaphore(name))

    def block(self):
        return BlockWrap(self.stack.enter_context(self.nc.Block()),
                         self.bar, self.bar_base, self.counter)


RELAX_SELF = False


class ESeq:

    def __init__(self, eng, sem, relax=False):
        self.eng, self.sem, self.n = eng, sem, 0
        self.relax = relax or RELAX_SELF

    def d(self, ins):
        ins.then_inc(self.sem, 1)
        self.n += 1
        if not self.relax:
            self.eng.wait_ge(self.sem, self.n)
        return ins


TT = 4096
D = 1024
NT = TT // 128
NC = D // 128
NCORE = 8


def emit_stageA(nc, x, w1, b1r, w23, b23r, idn, R1_out, m1_out, fuse=None):
    VP = 6
    VM1, VM2B, VM2C = {}, {}, {}
    _n = VP
    for _i in range(NT + 1):
        if _i < NT:
            _n += 1; VM1[_i] = _n
        if _i >= 1:
            _n += 8; VM2B[_i - 1] = _n
            _n += 3; VM2C[_i - 1] = _n
    VEND_A = _n
    vm1 = lambda i: VM1[i]
    vm2b = lambda i: VM2B[i]
    vm2c = lambda i: VM2C[i]
    ST = 9
    sm1 = lambda i: ST * i + 1
    sm2a = lambda i: ST * i + 6
    sm2b = lambda i: ST * i + 9
    with contextlib.ExitStack() as _stk:
        al = Alloc(nc, _stk, **(fuse or {}))
        idb = al.sb("a_idb", [128, 128], BF16)
        w1b = al.sb("a_w1b", [128, NC, D], BF16)
        w23b = al.sb("a_w23b", [128, NC, 32], BF16)
        b1b = al.sb("a_b1b", [1, D], BF16)
        b23b = al.sb("a_b23b", [1, 32], BF16)
        ones1 = al.sb("a_ones", [1, 128], BF16)
        xf = al.sb("a_xf", [128, 2, D], F32)
        xb = al.sb("a_xb", [128, 2, D], BF16)
        xT = al.sb("a_xT", [128, 2, D], BF16)
        tmp_r = al.sb("a_tr", [128, 2, D], F32)
        tmp_a = al.sb("a_ta", [128, 2, D], F32)
        tmp_e = al.sb("a_te", [128, 2, D], F32)
        tmp_l = al.sb("a_tl", [128, 2, D], F32)
        u23s = al.sb("a_u23", [128, 2, 32], F32)
        r1b = al.sb("a_r1", [128, 2, D], F32)
        cols = al.sb("a_cols", [128, 2, 8], F32)
        m1run = al.sb("a_m1run", [128, 1], F32)
        m1row = al.sb("a_m1row", [1, 128], F32)
        m1val = al.sb("a_m1val", [1, 1], F32)
        m1v8 = al.sb("a_m1v8", [1, 8], F32)
        csts = al.sb("a_csts", [128, 2], F32)
        trip = al.sb("a_trip", [1, 1], F32)
        ptr = al.ps("a_ptr", [128, 2, D], BF16)
        pu1 = al.ps("a_pu1", [128, 2, D], F32)
        pu23 = al.ps("a_pu23", [128, 2, 512], F32)
        colbuf = al.dram("a_colbuf", [128, 1], F32)
        ld0 = al.sem("a_ld0"); ld1 = al.sem("a_ld1"); lw = al.sem("a_lw")
        t1 = al.sem("a_t1"); t2 = al.sem("a_t2")
        st0 = al.sem("a_st0"); st1 = al.sem("a_st1")
        vself = al.sem("a_vself"); sself = al.sem("a_sself")
        ld = [ld0, ld1]; st = [st0, st1]
        block = al.block()

        @block.sync
        def _(sync):
            for i in range(NT):
                if i >= 2:
                    sync.wait_ge(vself, vm2b(i - 2))
                sync.dma_start(out=xf[:, i % 2, :], in_=x[128 * i:128 * (i + 1), :]).then_inc(ld[i % 2], 16)

        @block.gpsimd
        def _(gp):
            gp.dma_start(out=w1b[:], in_=w1.rearrange("(c p) o -> p c o", p=128)).then_inc(lw, 16)
            gp.dma_start(out=w23b[:], in_=w23.rearrange("(c p) o -> p c o", p=128)).then_inc(lw, 16)
            gp.dma_start(out=b1b[:], in_=b1r).then_inc(lw, 16)
            gp.dma_start(out=b23b[:], in_=b23r).then_inc(lw, 16)
            gp.dma_start(out=idb[:], in_=idn).then_inc(lw, 16)
            for i in range(NT):
                gp.wait_ge(vself, vm2c(i))
                if i >= 2:
                    gp.wait_ge(st[i % 2], 16 * (i // 2))
                gp.dma_start(out=R1_out[128 * i:128 * (i + 1), :], in_=r1b[:, i % 2, :]).then_inc(st[i % 2], 16)
            gp.dma_start(out=colbuf, in_=m1run[:, 0:1]).then_inc(lw, 16)
            gp.wait_ge(lw, 96)
            gp.dma_start(out=m1row[:], in_=colbuf.rearrange("p one -> one p")).then_inc(lw, 16)

        @block.vector
        def _(vector):
            V = ESeq(vector, vself)
            V.d(vector.memset(m1run[:], 0.0))
            V.d(vector.memset(trip[:], 0.0))
            V.d(vector.tensor_copy(trip[:].bitcast(I32), trip[:]))
            V.d(vector.memset(csts[:, 0:1], 1.0 / D))
            V.d(vector.memset(csts[:, 1:2], 1e-5))
            V.d(vector.memset(ones1[:], 1.0))
            for it in range(NT + 1):
                if it < NT:
                    i, k2 = it, it % 2
                    vector.wait_ge(ld[k2], 16 * (i // 2 + 1))
                    if i >= 2:
                        vector.wait_ge(t1, i - 1)
                    V.d(vector.tensor_copy(xb[:, k2, :], xf[:, k2, :]))
                    assert V.n == vm1(i)
                if it >= 1:
                    i, k2 = it - 1, (it - 1) % 2
                    vector.wait_ge(sself, sm2a(i))
                    if i >= 2:
                        vector.wait_ge(st[k2], 16 * (i // 2))
                    V.d(vector.tensor_add(tmp_l[:, k2, :], tmp_r[:, k2, :], tmp_l[:, k2, :]))
                    V.d(vector.tensor_mul(tmp_a[:, k2, 0:16], u23s[:, k2, 0:16], u23s[:, k2, 16:32]))
                    V.d(vector.tensor_reduce(cols[:, k2, 0:1], tmp_a[:, k2, 0:16], AX.X, ALU.add))
                    V.d(vector.tensor_scalar(tmp_l[:, k2, :], tmp_l[:, k2, :], cols[:, k2, 0:1], None, ALU.mult))
                    V.d(vector.scalar_tensor_tensor(r1b[:, k2, :], tmp_l[:, k2, :], 1.0, xf[:, k2, :], ALU.add, ALU.mult))
                    V.d(vector.tensor_reduce(cols[:, k2, 1:2], r1b[:, k2, :], AX.X, ALU.add, negate=True))
                    V.d(vector.tensor_scalar(cols[:, k2, 1:2], cols[:, k2, 1:2], 1.0 / D, None, ALU.mult))
                    V.d(vector.tensor_scalar(r1b[:, k2, :], r1b[:, k2, :], cols[:, k2, 1:2], None, ALU.add))
                    assert V.n == vm2b(i)
                    vector.wait_ge(sself, sm2b(i))
                    V.d(vector.tensor_scalar(r1b[:, k2, :], r1b[:, k2, :], cols[:, k2, 3:4], None, ALU.mult))
                    V.d(vector.tensor_reduce(cols[:, k2, 4:5], r1b[:, k2, :], AX.X, ALU.max, apply_absolute_value=True))
                    V.d(vector.tensor_max(m1run[:], m1run[:], cols[:, k2, 4:5]))
                    assert V.n == vm2c(i)
            vector.wait_ge(lw, 112)
            V.d(vector.tensor_reduce(m1val[:], m1row[:], AX.X, ALU.max))
            V.d(vector.tensor_copy(m1v8[:], m1val[0:1, 0:1].broadcast_to((1, 8))))

        @block.tensor
        def _(tensor):
            tensor.wait_ge(lw, 80)
            for i in range(NT):
                k2 = i % 2
                if i >= 1:
                    for s in range(8):
                        tensor.wait_ge(vself, min(vm1(i) + s, vm2b(i - 1)))
                        for _ in range(4):
                            tensor.ldweights(idb[:])
                tensor.wait_ge(vself, vm1(i))
                if i >= 2:
                    tensor.wait_ge(sself, sm1(i - 1))
                for c in range(NC):
                    ins = tensor.transpose(ptr[:, k2, 128 * c:128 * (c + 1)], xb[:, k2, 128 * c:128 * (c + 1)], idb[:])
                    if c == NC - 1:
                        ins.then_inc(t1, 1)
                if i >= 1:
                    tensor.wait_ge(sself, sm2b(i - 1))
                    for _ in range(4):
                        tensor.ldweights(idb[:])
                tensor.wait_ge(sself, sm1(i))
                if i >= 2:
                    tensor.wait_ge(sself, sm2a(i - 2))
                for h in range(2):
                    tensor.matmul(pu1[:, k2, 512 * h:512 * (h + 1)], ones1[0:1, :],
                                  b1b[0:1, 512 * h:512 * (h + 1)], start=True, stop=False)
                    for c in range(NC):
                        tensor.matmul(pu1[:, k2, 512 * h:512 * (h + 1)], xT[:, k2, 128 * c:128 * (c + 1)],
                                      w1b[:, c, 512 * h:512 * (h + 1)], start=False, stop=(c == NC - 1))
                tensor.matmul(pu23[:, k2, 0:32], ones1[0:1, :], b23b[0:1, :], start=True, stop=False)
                for c in range(NC):
                    ins = tensor.matmul(pu23[:, k2, 0:32], xT[:, k2, 128 * c:128 * (c + 1)], w23b[:, c, :],
                                        start=False, stop=(c == NC - 1))
                    if c == NC - 1:
                        ins.then_inc(t2, 1)

        @block.scalar
        def _(scalar):
            S = ESeq(scalar, sself)
            for i in range(NT):
                k2 = i % 2
                scalar.wait_ge(t1, i + 1)
                if i >= 2:
                    scalar.wait_ge(t2, i - 1)
                S.d(scalar.copy(xT[:, k2, :], ptr[:, k2, :]))
                assert S.n == sm1(i)
                scalar.wait_ge(t2, i + 1)
                if i >= 2:
                    scalar.wait_ge(vself, vm2c(i - 2))
                S.d(scalar.activation(out=tmp_r[:, k2, :], in_=pu1[:, k2, :], func=AF.Relu))
                S.d(scalar.activation(out=tmp_a[:, k2, :], in_=pu1[:, k2, :], func=AF.Abs))
                S.d(scalar.activation(out=tmp_e[:, k2, :], in_=tmp_a[:, k2, :], func=AF.Exp, scale=-1.0))
                S.d(scalar.activation(out=tmp_l[:, k2, :], in_=tmp_e[:, k2, :], func=AF.Ln, bias=1.0))
                S.d(scalar.copy(u23s[:, k2, :], pu23[:, k2, 0:32]))
                assert S.n == sm2a(i)
                scalar.wait_ge(vself, vm2b(i))
                S.d(scalar.activation(out=tmp_e[:, k2, :], in_=r1b[:, k2, :], func=AF.Square, accum_out=cols[:, k2, 2:3]))
                S.d(scalar.activation(out=cols[:, k2, 3:4], in_=cols[:, k2, 2:3], func=AF.Ln, scale=csts[:, 0:1], bias=csts[:, 1:2]))
                S.d(scalar.activation(out=cols[:, k2, 3:4], in_=cols[:, k2, 3:4], func=AF.Exp, scale=-0.5))
                assert S.n == sm2b(i)

        @block.gpsimd
        def _(gp):
            gp.wait_ge(vself, VEND_A + 2)
            gp.dma_start(out=m1_out, in_=m1v8[:]).then_inc(lw, 16)
            gp.wait_ge(lw, 128)
            gp.wait_ge(st0, 16 * (NT // 2))
            gp.wait_ge(st1, 16 * (NT // 2))
    return nc



NOC = 3072 // 128


def emit_stageB1(nc, R1, m1s, w_up, w_gate, idn, raw_up, raw_gate, m2u_out, m2g_out, fuse=None):
    VQ = 3
    BP = 7
    vq_m = lambda i: BP + VQ * i + VQ
    VWM, VAM = {}, {}
    _n = BP + VQ * NT
    for _j in range(2 * NOC + 1):
        if _j < 2 * NOC:
            _n += 8; VWM[_j] = _n
        if _j >= 1:
            _n += 10; VAM[_j - 1] = _n
    VEND_B1 = _n
    vw_m = lambda j: VWM[j]
    va_m = lambda j: VAM[j]
    sq_m = lambda i: i + 1
    sw_m = lambda j: NT + 9 * j + 1
    se_m = lambda j, tb: NT + 9 * j + 2 + tb
    tq_m = lambda i: i + 1
    tw_m = lambda j: NT + 9 * j + 1
    tm_m = lambda j, tb: NT + 9 * j + 2 + tb
    with contextlib.ExitStack() as _stk:
        al = Alloc(nc, _stk, **(fuse or {}))
        idb = al.sb("b_idb", [128, 128], BF16)
        rf = al.sb("b_rf", [128, 2, D], F32)
        vt = al.sb("b_vt", [128, 2, D], F32)
        xqb = al.sb("b_xqb", [128, 2, D], BF16)
        xqT = al.sb("b_xqT", [128, NC, TT], BF16)
        wst = al.sb("b_wst", [128, 2, D], F32)
        wqb = al.sb("b_wqb", [128, 2, D], BF16)
        wqT = al.sb("b_wqT", [128, 2, D], BF16)
        rawb = al.sb("b_rawb", [128, 2, TT], F32)
        bq = al.sb("b_bq", [128, 2, 3, 32, 1], F32)
        amax = al.sb("b_amax", [128, 2, 10], F32)
        gself = al.sem("b_gself")
        scs = al.sb("b_scs", [128, 2], F32)
        m1r = al.sb("b_m1r", [1, 8], F32)
        sc1 = al.sb("b_sc1", [1, 4], F32)
        mrun = al.sb("b_mrun", [128, 2], F32)
        mrow = al.sb("b_mrow", [1, 128], F32)
        m2v = al.sb("b_m2v", [1, 2], F32)
        m2v8 = al.sb("b_m2v8", [1, 16], F32)
        trip = al.sb("b_trip", [1, 1], F32)
        ptr = al.ps("b_ptr", [128, 2, D], BF16)
        ptw = al.ps("b_ptw", [128, 2, D], BF16)
        praw = al.ps("b_praw", [128, 2, 512], F32)
        scal_d = al.dram("b_scal", [1, 2], F32)
        colb_d = al.dram("b_colb", [128, 2], F32)
        ldr0 = al.sem("b_ldr0"); ldr1 = al.sem("b_ldr1")
        lw0 = al.sem("b_lw0"); lw1 = al.sem("b_lw1")
        st0 = al.sem("b_st0"); st1 = al.sem("b_st1")
        aux = al.sem("b_aux")
        vself = al.sem("b_vself"); sself = al.sem("b_sself"); tself = al.sem("b_tself")
        ldr = [ldr0, ldr1]; lw = [lw0, lw1]; st = [st0, st1]
        block = al.block()
        WMAT = [w_up, w_gate]

        @block.gpsimd
        def _(gp):
            gp.dma_start(out=idb[:], in_=idn).then_inc(aux, 16)
            gp.dma_start(out=m1r[:], in_=m1s).then_inc(aux, 16)
            gp.wait_ge(vself, 4)
            gp.dma_start(out=scal_d, in_=sc1[0:1, 1:3]).then_inc(aux, 16)
            gp.wait_ge(aux, 48)
            gp.dma_start(out=scs[:, 0:2], in_=scal_d.broadcast_to((128, 2))).then_inc(aux, 16)
            for i in range(NT):
                if i >= 2:
                    gp.wait_ge(vself, vq_m(i - 2))
                gp.dma_start(out=rf[:, i % 2, :], in_=R1[128 * i:128 * (i + 1), :]).then_inc(ldr[i % 2], 16)
            for j in range(2 * NOC):
                mi, oc = divmod(j, NOC)
                if j >= 2:
                    gp.wait_ge(st[j % 2], 16 * (j // 2))
                gp.wait_ge(vself, va_m(j))
                dst = raw_up if mi == 0 else raw_gate
                gp.dma_start(out=dst[128 * oc:128 * (oc + 1), :], in_=rawb[:, j % 2, :]).then_inc(st[j % 2], 16)
            gp.dma_start(out=colb_d, in_=mrun[:, 0:2]).then_inc(aux, 16)
            gp.wait_ge(aux, 80)
            gp.dma_start(out=mrow[:], in_=colb_d[:, 0:1].rearrange("p one -> one p"), allow_slow_non_contiguous=True).then_inc(aux, 16)
            gp.wait_ge(vself, VEND_B1 + 1)
            gp.dma_start(out=mrow[:], in_=colb_d[:, 1:2].rearrange("p one -> one p"), allow_slow_non_contiguous=True).then_inc(aux, 16)
            gp.wait_ge(vself, VEND_B1 + 4)
            gp.dma_start(out=m2u_out, in_=m2v8[0:1, 0:8]).then_inc(aux, 16)
            gp.dma_start(out=m2g_out, in_=m2v8[0:1, 8:16]).then_inc(aux, 16)
            gp.wait_ge(aux, 144)
            gp.wait_ge(st0, 16 * NOC)
            gp.wait_ge(st1, 16 * NOC)

        @block.sync
        def _(sync):
            for j in range(2 * NOC):
                mi, oc = divmod(j, NOC)
                if j >= 2:
                    sync.wait_ge(vself, vw_m(j - 2))
                sync.dma_start(out=wst[:, j % 2, :], in_=WMAT[mi][128 * oc:128 * (oc + 1), :]).then_inc(lw[j % 2], 16)

        @block.vector
        def _(vector):
            V = ESeq(vector, vself)
            vector.wait_ge(aux, 32)
            V.d(vector.tensor_reduce(sc1[0:1, 0:1], m1r[:], AX.X, ALU.max))
            V.d(vector.tensor_scalar(sc1[0:1, 1:2], sc1[0:1, 0:1], 1.0 / 32767.0, None, ALU.mult))
            V.d(vector.tensor_scalar(sc1[0:1, 1:2], sc1[0:1, 1:2], 1e-12, None, ALU.max))
            V.d(vector.reciprocal(sc1[0:1, 2:3], sc1[0:1, 1:2]))
            V.d(vector.memset(mrun[:], 0.0))
            V.d(vector.memset(trip[:], 0.0))
            V.d(vector.tensor_copy(trip[:].bitcast(I32), trip[:]))
            for i in range(NT):
                k2 = i % 2
                vector.wait_ge(ldr[k2], 16 * (i // 2 + 1))
                vector.wait_ge(aux, 64)
                if i >= 2:
                    vector.wait_ge(tself, tq_m(i - 2))
                V.d(vector.tensor_scalar(vt[:, k2, :], rf[:, k2, :], scs[:, 1:2], RC_MAGIC, ALU.mult, ALU.add))
                V.d(vector.tensor_scalar(vt[:, k2, :], vt[:, k2, :], -RC_MAGIC, None, ALU.add))
                V.d(vector.tensor_scalar(xqb[:, k2, :], vt[:, k2, :], scs[:, 0:1], None, ALU.mult))
                assert V.n == vq_m(i)
            for it in range(2 * NOC + 1):
                if it < 2 * NOC:
                    j = it
                    k2 = j % 2
                    vector.wait_ge(lw[k2], 16 * (j // 2 + 1))
                    if j >= 2:
                        vector.wait_ge(tself, tw_m(j - 2))
                    w3 = wst[:, k2, :].rearrange("p (b k) -> p b k", k=32)
                    wq3 = wqb[:, k2, :].rearrange("p (b k) -> p b k", k=32)
                    V.d(vector.tensor_reduce(bq[:, k2, 0], w3, AX.X, ALU.max, apply_absolute_value=True))
                    V.d(vector.tensor_scalar(bq[:, k2, 1], bq[:, k2, 0], 1.0 / 7.0, None, ALU.mult))
                    V.d(vector.tensor_scalar(bq[:, k2, 1], bq[:, k2, 1], 1e-12, None, ALU.max))
                    V.d(vector.reciprocal(bq[:, k2, 2], bq[:, k2, 1]))
                    V.d(vector.tensor_tensor(w3, w3, bq[:, k2, 2].broadcast_to((128, 32, 32)), ALU.mult))
                    V.d(vector.tensor_scalar(wst[:, k2, :], wst[:, k2, :], RC_MAGIC, None, ALU.add))
                    V.d(vector.tensor_scalar(wst[:, k2, :], wst[:, k2, :], -RC_MAGIC, None, ALU.add))
                    V.d(vector.tensor_tensor(wq3, w3, bq[:, k2, 1].broadcast_to((128, 32, 32)), ALU.mult))
                    assert V.n == vw_m(j)
                if it >= 1:
                    j = it - 1
                    k2 = j % 2
                    mi = j // NOC
                    for tb in range(8):
                        vector.wait_ge(sself, se_m(j, tb))
                        V.d(vector.tensor_reduce(amax[:, k2, tb:tb + 1], rawb[:, k2, 512 * tb:512 * (tb + 1)], AX.X, ALU.max, apply_absolute_value=True))
                    V.d(vector.tensor_reduce(amax[:, k2, 8:9], amax[:, k2, 0:8], AX.X, ALU.max))
                    V.d(vector.tensor_max(mrun[:, mi:mi + 1], mrun[:, mi:mi + 1], amax[:, k2, 8:9]))
                    assert V.n == va_m(j)
            vector.wait_ge(aux, 96)
            V.d(vector.tensor_reduce(m2v[0:1, 0:1], mrow[:], AX.X, ALU.max))
            vector.wait_ge(aux, 112)
            V.d(vector.tensor_reduce(m2v[0:1, 1:2], mrow[:], AX.X, ALU.max))
            V.d(vector.tensor_copy(m2v8[0:1, 0:8], m2v[0:1, 0:1].broadcast_to((1, 8))))
            V.d(vector.tensor_copy(m2v8[0:1, 8:16], m2v[0:1, 1:2].broadcast_to((1, 8))))

        @block.tensor
        def _(tensor):
            T = ESeq(tensor, tself, relax=True)
            tensor.wait_ge(aux, 32)
            for i in range(NT):
                k2 = i % 2
                if i >= 1:
                    for s in range(3):
                        tensor.wait_ge(vself, vq_m(i - 1) + s + 1)
                        for _ in range(4):
                            tensor.ldweights(idb[:])
                tensor.wait_ge(vself, vq_m(i))
                if i >= 2:
                    tensor.wait_ge(sself, sq_m(i - 2))
                for c in range(NC):
                    ins = tensor.transpose(ptr[:, k2, 128 * c:128 * (c + 1)], xqb[:, k2, 128 * c:128 * (c + 1)], idb[:])
                T.d(ins)
                assert T.n == tq_m(i)
            for j in range(2 * NOC):
                k2 = j % 2
                tensor.wait_ge(vself, vw_m(j))
                if j >= 2:
                    tensor.wait_ge(sself, sw_m(j - 2))
                for c in range(NC):
                    ins = tensor.transpose(ptw[:, k2, 128 * c:128 * (c + 1)], wqb[:, k2, 128 * c:128 * (c + 1)], idb[:])
                T.d(ins)
                tensor.wait_ge(sself, sw_m(j))
                for tb in range(8):
                    prev = 8 * j + tb - 2
                    if prev >= 0:
                        tensor.wait_ge(sself, se_m(prev // 8, prev % 8))
                    for c in range(NC):
                        ins = tensor.matmul(praw[:, tb % 2, :], wqT[:, k2, 128 * c:128 * (c + 1)],
                                            xqT[:, c, 512 * tb:512 * (tb + 1)], start=(c == 0), stop=(c == NC - 1))
                    T.d(ins)
                assert T.n == tm_m(j, 7)

        @block.scalar
        def _(scalar):
            S = ESeq(scalar, sself)
            for i in range(NT):
                k2 = i % 2
                scalar.wait_ge(tself, tq_m(i))
                S.d(scalar.copy(xqT[:, :, 128 * i:128 * (i + 1)], ptr[:, k2, :].rearrange("p (c t) -> p c t", t=128)))
                assert S.n == sq_m(i)
            for j in range(2 * NOC):
                k2 = j % 2
                scalar.wait_ge(tself, tw_m(j))
                if j >= 2:
                    scalar.wait_ge(tself, tm_m(j - 2, 7))
                S.d(scalar.copy(wqT[:, k2, :], ptw[:, k2, :]))
                assert S.n == sw_m(j)
                for tb in range(8):
                    scalar.wait_ge(tself, tm_m(j, tb))
                    if j >= 2:
                        scalar.wait_ge(st[k2], 16 * (j // 2))
                    S.d(scalar.copy(rawb[:, k2, 512 * tb:512 * (tb + 1)], praw[:, tb % 2, :]))
                    assert S.n == se_m(j, tb)
    return nc


def build_stageA():
    nc = bass.Bass("TRN2", target_bir_lowering=False)
    x = nc.dram_tensor("x", [TT, D], F32, kind="ExternalInput").ap()
    w1 = nc.dram_tensor("w1", [D, D], F32, kind="ExternalInput").ap()
    b1r = nc.dram_tensor("b1r", [1, D], F32, kind="ExternalInput").ap()
    w23 = nc.dram_tensor("w23", [D, 32], F32, kind="ExternalInput").ap()
    b23r = nc.dram_tensor("b23r", [1, 32], F32, kind="ExternalInput").ap()
    idn = nc.dram_tensor("idn", [128, 128], F32, kind="ExternalInput").ap()
    R1 = nc.dram_tensor("R1", [TT, D], F32, kind="ExternalOutput").ap()
    m1 = nc.dram_tensor("m1", [1, 8], F32, kind="ExternalOutput").ap()
    emit_stageA(nc, x, w1, b1r, w23, b23r, idn, R1, m1)
    return nc



def emit_stageB2(nc, raw_gate, m2g, p_out, m3_out, fuse=None):
    BP2 = 10
    vc_m = lambda j: BP2 + 5 * j + 5
    sc_m = lambda j: 4 * j + 4
    with contextlib.ExitStack() as _stk:
        al = Alloc(nc, _stk, **(fuse or {}))
        gin = al.sb("c2_gin", [128, 2, TT], F32)
        t1 = al.sb("c2_t1", [128, 2, TT], F32)
        t2 = al.sb("c2_t2", [128, 2, TT], F32)
        pbuf = al.sb("c2_pbuf", [128, 2, TT], F32)
        csts = al.sb("c2_csts", [128, 8], F32)
        m2r = al.sb("c2_m2r", [1, 8], F32)
        sc1 = al.sb("c2_sc1", [1, 4], F32)
        mrun = al.sb("c2_mrun", [128, 1], F32)
        mrow = al.sb("c2_mrow", [1, 128], F32)
        m3v = al.sb("c2_m3v", [1, 1], F32)
        m3v8 = al.sb("c2_m3v8", [1, 8], F32)
        trip = al.sb("c2_trip", [1, 1], F32)
        amax = al.sb("c2_amax", [128, 2], F32)
        scal_d = al.dram("c2_scal", [1, 3], F32)
        colb_d = al.dram("c2_colb", [128, 1], F32)
        lg0 = al.sem("c2_lg0"); lg1 = al.sem("c2_lg1")
        sp0 = al.sem("c2_sp0"); sp1 = al.sem("c2_sp1")
        aux = al.sem("c2_aux")
        vself = al.sem("c2_vself"); sself = al.sem("c2_sself")
        lg = [lg0, lg1]; sp = [sp0, sp1]
        register_const(nc, RC_MAGIC, csts[:, 2:3])
        register_const(nc, -RC_MAGIC, csts[:, 3:4])
        block = al.block()

        @block.gpsimd
        def _(gp):
            gp.dma_start(out=m2r[:], in_=m2g).then_inc(aux, 16)
            gp.wait_ge(vself, 5)
            gp.dma_start(out=scal_d, in_=sc1[0:1, 1:4]).then_inc(aux, 16)
            gp.wait_ge(aux, 32)
            gp.dma_start(out=csts[:, 4:7], in_=scal_d.broadcast_to((128, 3))).then_inc(aux, 16)
            for j in range(NOC):
                if j >= 2:
                    gp.wait_ge(vself, vc_m(j - 2))
                gp.dma_start(out=gin[:, j % 2, :], in_=raw_gate[128 * j:128 * (j + 1), :]).then_inc(lg[j % 2], 16)
            gp.wait_ge(vself, vc_m(NOC - 1))
            gp.dma_start(out=colb_d, in_=mrun[:, 0:1]).then_inc(aux, 16)
            gp.wait_ge(aux, 64)
            gp.dma_start(out=mrow[:], in_=colb_d.rearrange("p one -> one p")).then_inc(aux, 16)
            gp.wait_ge(vself, vc_m(NOC - 1) + 2)
            gp.dma_start(out=m3_out, in_=m3v8[:]).then_inc(aux, 16)
            gp.wait_ge(aux, 96)

        @block.sync
        def _(sync):
            for j in range(NOC):
                sync.wait_ge(vself, vc_m(j))
                if j >= 2:
                    sync.wait_ge(sp[j % 2], 16 * (j // 2))
                sync.dma_start(out=p_out[128 * j:128 * (j + 1), :], in_=pbuf[:, j % 2, :]).then_inc(sp[j % 2], 16)

        @block.vector
        def _(vector):
            V = ESeq(vector, vself)
            vector.wait_ge(aux, 16)
            V.d(vector.tensor_reduce(sc1[0:1, 0:1], m2r[:], AX.X, ALU.max))
            V.d(vector.tensor_scalar(sc1[0:1, 1:2], sc1[0:1, 0:1], 1.0 / 32767.0, None, ALU.mult))
            V.d(vector.tensor_scalar(sc1[0:1, 1:2], sc1[0:1, 1:2], 1e-12, None, ALU.max))
            V.d(vector.reciprocal(sc1[0:1, 2:3], sc1[0:1, 1:2]))
            V.d(vector.tensor_scalar(sc1[0:1, 3:4], sc1[0:1, 1:2], 0.5, None, ALU.mult))
            V.d(vector.memset(mrun[:], 0.0))
            V.d(vector.memset(trip[:], 0.0))
            V.d(vector.tensor_copy(trip[:].bitcast(I32), trip[:]))
            V.d(vector.memset(csts[:, 2:3], RC_MAGIC))
            V.d(vector.memset(csts[:, 3:4], -RC_MAGIC))
            for j in range(NOC):
                k2 = j % 2
                vector.wait_ge(sself, sc_m(j))
                if j >= 2:
                    vector.wait_ge(sp[k2], 16 * (j // 2))
                V.d(vector.tensor_scalar(t2[:, k2, :], t2[:, k2, :], -RC_MAGIC, None, ALU.add))
                V.d(vector.tensor_scalar(t2[:, k2, :], t2[:, k2, :], 65535.0, 2.0 ** -16, ALU.min, ALU.mult))
                V.d(vector.scalar_tensor_tensor(pbuf[:, k2, :], t1[:, k2, :], csts[:, 4:5], t2[:, k2, :], ALU.mult, ALU.mult))
                V.d(vector.tensor_reduce(amax[:, k2:k2 + 1], pbuf[:, k2, :], AX.X, ALU.max, apply_absolute_value=True))
                V.d(vector.tensor_max(mrun[:], mrun[:], amax[:, k2:k2 + 1]))
                assert V.n == vc_m(j)
            vector.wait_ge(aux, 80)
            V.d(vector.tensor_reduce(m3v[:], mrow[:], AX.X, ALU.max))
            V.d(vector.tensor_copy(m3v8[:], m3v[0:1, 0:1].broadcast_to((1, 8))))

        @block.scalar
        def _(scalar):
            S = ESeq(scalar, sself)
            for j in range(NOC):
                k2 = j % 2
                scalar.wait_ge(lg[k2], 16 * (j // 2 + 1))
                scalar.wait_ge(vself, BP2)
                scalar.wait_ge(aux, 48)
                if j >= 2:
                    scalar.wait_ge(vself, vc_m(j - 2))
                S.d(scalar.activation(out=t1[:, k2, :], in_=gin[:, k2, :], func=AF.Copy, scale=csts[:, 5:6], bias=RC_MAGIC))
                S.d(scalar.activation(out=t1[:, k2, :], in_=t1[:, k2, :], func=AF.Copy, bias=-RC_MAGIC))
                S.d(scalar.activation(out=t2[:, k2, :], in_=t1[:, k2, :], func=AF.Tanh, scale=csts[:, 6:7]))
                S.d(scalar.activation(out=t2[:, k2, :], in_=t2[:, k2, :], func=AF.Copy, scale=32768.0, bias=32768.0 + RC_MAGIC))
                assert S.n == sc_m(j)

        @block.tensor
        def _(tensor):
            for j in range(NOC):
                for s in range(5):
                    tensor.wait_ge(vself, BP2 + 5 * j + s + 1)
                    for _ in range(28):
                        tensor.ldweights(csts[:, 2:7].bitcast(BF16))
    return nc


def emit_stageB3(nc, p_in, raw_up, m2u, m3s, o_out, m4_out, fuse=None):
    BP3 = 13
    vc_m = lambda j: BP3 + 3 * j + 3
    sc_m = lambda j: 3 * j + 3
    with contextlib.ExitStack() as _stk:
        al = Alloc(nc, _stk, **(fuse or {}))
        pin = al.sb("c3_pin", [128, 2, TT], F32)
        uin = al.sb("c3_uin", [128, 2, TT], F32)
        t1 = al.sb("c3_t1", [128, 2, TT], F32)
        t2 = al.sb("c3_t2", [128, 2, TT], F32)
        obuf = al.sb("c3_obuf", [128, 2, TT], F32)
        csts = al.sb("c3_csts", [128, 8], F32)
        m2r = al.sb("c3_m2r", [1, 8], F32)
        m3r = al.sb("c3_m3r", [1, 8], F32)
        sc1 = al.sb("c3_sc1", [1, 8], F32)
        mrun = al.sb("c3_mrun", [128, 1], F32)
        mrow = al.sb("c3_mrow", [1, 128], F32)
        m4v = al.sb("c3_m4v", [1, 1], F32)
        m4v16 = al.sb("c3_m4v16", [1, 16], F32)
        trip = al.sb("c3_trip", [1, 1], F32)
        amax = al.sb("c3_amax", [128, 2], F32)
        scal_d = al.dram("c3_scal", [1, 2], F32)
        colb_d = al.dram("c3_colb", [128, 1], F32)
        lp0 = al.sem("c3_lp0"); lp1 = al.sem("c3_lp1")
        lu0 = al.sem("c3_lu0"); lu1 = al.sem("c3_lu1")
        so0 = al.sem("c3_so0"); so1 = al.sem("c3_so1")
        aux = al.sem("c3_aux")
        vself = al.sem("c3_vself"); sself = al.sem("c3_sself")
        lp = [lp0, lp1]; lu = [lu0, lu1]; so = [so0, so1]
        register_const(nc, RC_MAGIC, csts[:, 3:4])
        register_const(nc, -RC_MAGIC, csts[:, 4:5])
        block = al.block()

        @block.gpsimd
        def _(gp):
            gp.dma_start(out=m2r[:], in_=m2u).then_inc(aux, 16)
            gp.dma_start(out=m3r[:], in_=m3s).then_inc(aux, 16)
            gp.wait_ge(vself, 9)
            gp.dma_start(out=scal_d, in_=sc1[0:1, 2:4]).then_inc(aux, 16)
            gp.wait_ge(aux, 48)
            gp.dma_start(out=csts[:, 0:2], in_=scal_d.broadcast_to((128, 2))).then_inc(aux, 16)
            for j in range(NOC):
                if j >= 2:
                    gp.wait_ge(vself, vc_m(j - 2))
                gp.dma_start(out=pin[:, j % 2, :], in_=p_in[128 * j:128 * (j + 1), :]).then_inc(lp[j % 2], 16)
                if j >= 1:
                    jo = j - 1
                    gp.wait_ge(vself, vc_m(jo))
                    if jo >= 2:
                        gp.wait_ge(so[jo % 2], 16 * (jo // 2))
                    gp.dma_start(out=o_out[128 * jo:128 * (jo + 1), :], in_=obuf[:, jo % 2, :]).then_inc(so[jo % 2], 16)
            jo = NOC - 1
            gp.wait_ge(vself, vc_m(jo))
            gp.wait_ge(so[jo % 2], 16 * (jo // 2))
            gp.dma_start(out=o_out[128 * jo:128 * (jo + 1), :], in_=obuf[:, jo % 2, :]).then_inc(so[jo % 2], 16)
            gp.wait_ge(vself, vc_m(NOC - 1))
            gp.dma_start(out=colb_d, in_=mrun[:, 0:1]).then_inc(aux, 16)
            gp.wait_ge(aux, 80)
            gp.dma_start(out=mrow[:], in_=colb_d.rearrange("p one -> one p")).then_inc(aux, 16)
            gp.wait_ge(vself, vc_m(NOC - 1) + 3)
            gp.dma_start(out=m4_out, in_=m4v16[:]).then_inc(aux, 16)
            gp.wait_ge(aux, 112)

        @block.sync
        def _(sync):
            for j in range(NOC):
                if j >= 2:
                    sync.wait_ge(sself, sc_m(j - 2))
                sync.dma_start(out=uin[:, j % 2, :], in_=raw_up[128 * j:128 * (j + 1), :]).then_inc(lu[j % 2], 16)

        @block.gpsimd
        def _(gp2):
            gp2.wait_ge(so0, 16 * (NOC // 2))
            gp2.wait_ge(so1, 16 * (NOC // 2))

        @block.vector
        def _(vector):
            V = ESeq(vector, vself)
            vector.wait_ge(aux, 32)
            V.d(vector.tensor_reduce(sc1[0:1, 0:1], m3r[:], AX.X, ALU.max))
            V.d(vector.tensor_scalar(sc1[0:1, 1:2], sc1[0:1, 0:1], 1.0 / 32767.0, None, ALU.mult))
            V.d(vector.tensor_scalar(sc1[0:1, 1:2], sc1[0:1, 1:2], 1e-12, None, ALU.max))
            V.d(vector.reciprocal(sc1[0:1, 2:3], sc1[0:1, 1:2]))
            V.d(vector.tensor_reduce(sc1[0:1, 3:4], m2r[:], AX.X, ALU.max))
            V.d(vector.tensor_scalar(sc1[0:1, 4:5], sc1[0:1, 3:4], 1.0 / 32767.0, None, ALU.mult))
            V.d(vector.tensor_scalar(sc1[0:1, 4:5], sc1[0:1, 4:5], 1e-12, None, ALU.max))
            V.d(vector.reciprocal(sc1[0:1, 3:4], sc1[0:1, 4:5]))
            V.d(vector.tensor_mul(sc1[0:1, 4:5], sc1[0:1, 4:5], sc1[0:1, 1:2]))
            V.d(vector.memset(mrun[:], 0.0))
            V.d(vector.memset(trip[:], 0.0))
            V.d(vector.memset(csts[:, 3:4], RC_MAGIC))
            V.d(vector.memset(csts[:, 4:5], -RC_MAGIC))
            assert V.n == BP3
            for j in range(NOC):
                k2 = j % 2
                vector.wait_ge(sself, sc_m(j))
                if j >= 2:
                    vector.wait_ge(so[k2], 16 * (j // 2))
                V.d(vector.scalar_tensor_tensor(obuf[:, k2, :], t2[:, k2, :], RC_MAGIC, t1[:, k2, :], ALU.subtract, ALU.mult))
                V.d(vector.tensor_reduce(amax[:, k2:k2 + 1], obuf[:, k2, :], AX.X, ALU.max, apply_absolute_value=True))
                V.d(vector.tensor_max(mrun[:], mrun[:], amax[:, k2:k2 + 1]))
                assert V.n == vc_m(j)
            vector.wait_ge(aux, 96)
            V.d(vector.tensor_reduce(m4v[:], mrow[:], AX.X, ALU.max))
            V.d(vector.tensor_copy(m4v16[0:1, 0:8], m4v[0:1, 0:1].broadcast_to((1, 8))))
            V.d(vector.tensor_copy(m4v16[0:1, 8:16], sc1[0:1, 4:5].broadcast_to((1, 8))))

        @block.scalar
        def _(scalar):
            S = ESeq(scalar, sself)
            for j in range(NOC):
                k2 = j % 2
                scalar.wait_ge(lp[k2], 16 * (j // 2 + 1))
                scalar.wait_ge(lu[k2], 16 * (j // 2 + 1))
                scalar.wait_ge(vself, BP3)
                scalar.wait_ge(aux, 64)
                if j >= 2:
                    scalar.wait_ge(vself, vc_m(j - 2))
                S.d(scalar.activation(out=t1[:, k2, :], in_=pin[:, k2, :], func=AF.Copy, scale=csts[:, 0:1], bias=RC_MAGIC))
                S.d(scalar.activation(out=t1[:, k2, :], in_=t1[:, k2, :], func=AF.Copy, bias=-RC_MAGIC))
                S.d(scalar.activation(out=t2[:, k2, :], in_=uin[:, k2, :], func=AF.Copy, scale=csts[:, 1:2], bias=RC_MAGIC))
                assert S.n == sc_m(j)

        @block.tensor
        def _(tensor):
            for j in range(NOC):
                for s in range(3):
                    tensor.wait_ge(vself, BP3 + 3 * j + s + 1)
                    for _ in range(40):
                        tensor.ldweights(csts[:, 3:5].bitcast(BF16))
    return nc


def build_stageB1():
    nc = bass.Bass("TRN2", target_bir_lowering=False)
    R1 = nc.dram_tensor("R1", [TT, D], F32, kind="ExternalInput").ap()
    m1s = nc.dram_tensor("m1s", [1, 8], F32, kind="ExternalInput").ap()
    w_up = nc.dram_tensor("w_up", [3072, D], F32, kind="ExternalInput").ap()
    w_gate = nc.dram_tensor("w_gate", [3072, D], F32, kind="ExternalInput").ap()
    idn = nc.dram_tensor("idn", [128, 128], F32, kind="ExternalInput").ap()
    raw_up = nc.dram_tensor("raw_up", [3072, TT], F32, kind="ExternalOutput").ap()
    raw_gate = nc.dram_tensor("raw_gate", [3072, TT], F32, kind="ExternalOutput").ap()
    m2u = nc.dram_tensor("m2u_o", [1, 8], F32, kind="ExternalOutput").ap()
    m2g = nc.dram_tensor("m2g_o", [1, 8], F32, kind="ExternalOutput").ap()
    emit_stageB1(nc, R1, m1s, w_up, w_gate, idn, raw_up, raw_gate, m2u, m2g)
    return nc



def build_stageB2():
    nc = bass.Bass("TRN2", target_bir_lowering=False)
    raw_gate = nc.dram_tensor("raw_gate", [3072, TT], F32, kind="ExternalInput").ap()
    m2g = nc.dram_tensor("m2g", [1, 8], F32, kind="ExternalInput").ap()
    p_out = nc.dram_tensor("p_out", [3072, TT], F32, kind="ExternalOutput").ap()
    m3 = nc.dram_tensor("m3", [1, 8], F32, kind="ExternalOutput").ap()
    emit_stageB2(nc, raw_gate, m2g, p_out, m3)
    return nc


def build_stageB3():
    nc = bass.Bass("TRN2", target_bir_lowering=False)
    p_in = nc.dram_tensor("p_in", [3072, TT], F32, kind="ExternalInput").ap()
    raw_up = nc.dram_tensor("raw_up", [3072, TT], F32, kind="ExternalInput").ap()
    m2u = nc.dram_tensor("m2u", [1, 8], F32, kind="ExternalInput").ap()
    m3s = nc.dram_tensor("m3s", [1, 8], F32, kind="ExternalInput").ap()
    o_out = nc.dram_tensor("o_out", [3072, TT], F32, kind="ExternalOutput").ap()
    m4 = nc.dram_tensor("m4", [1, 16], F32, kind="ExternalOutput").ap()
    emit_stageB3(nc, p_in, raw_up, m2u, m3s, o_out, m4)
    return nc



def emit_stageB4(nc, o_pre, m4s, R1, w_down, idn, R2_out, fuse=None):
    VP4 = 75
    vdq_m = lambda hc: 11 + 8 * (hc + 1)
    VQM, VTM = {}, {}
    _n = VP4
    NTB_ = 8
    for _t in range(NTB_ + 1):
        if _t < NTB_:
            for _ic in range(NOC):
                _n += 1; VQM[(_t, _ic)] = _n
        if _t >= 1:
            for _t4 in range(4):
                for _k in range(1, 6):
                    _n += 1; VTM[(_t - 1, _t4, _k)] = _n
    vq_m = lambda tb, ic: VQM[(tb, ic)]
    vqall = lambda tb: VQM[(tb, NOC - 1)]
    vt_m = lambda tb, t4, k=5: VTM[(tb, t4, k)]
    SP4 = 8
    SQM, SEVT, SACTT = {}, {}, {}
    _s = SP4
    for _t in range(NTB_):
        for _g in range(8):
            for _ic in range(3 * _g, 3 * _g + 3):
                _s += 2; SQM[(_t, _ic)] = _s
            if _t >= 1:
                _s += 1; SEVT[(_t - 1, _g // 4, _g % 4)] = _s
        if _t >= 1:
            for _t4 in range(4):
                _s += 3; SACTT[(_t - 1, _t4)] = _s
    for _g in range(8):
        _s += 1; SEVT[(NTB_ - 1, _g // 4, _g % 4)] = _s
    for _t4 in range(4):
        _s += 3; SACTT[(NTB_ - 1, _t4)] = _s
    sq_m = lambda tb, ic: SQM[(tb, ic)]
    sevac_m = lambda tb, h2, tt4: SEVT[(tb, h2, tt4)]
    sact_m = lambda tb, t4: SACTT[(tb, t4)]
    TP4 = 8
    tp_m = lambda hc: hc + 1
    tm_m = lambda tb, h2, tt4: TP4 + 8 * tb + 4 * h2 + tt4 + 1
    NTB = 8
    with contextlib.ExitStack() as _stk:
        al = Alloc(nc, _stk, **(fuse or {}))
        idb = al.sb("d4_idb", [128, 128], BF16)
        wdst = al.sb("d4_wdst", [128, 2, 3072], F32)
        wdqb = al.sb("d4_wdqb", [128, 2, 3072], BF16)
        wdT = al.sb("d4_wdT", [128, NOC, D], BF16)
        bq = al.sb("d4_bq", [128, 2, 3, 96, 1], F32)
        oin = al.sb("d4_oin", [128, 4, 512], F32)
        ob = al.sb("d4_ob", [128, 2, NOC, 512], BF16)
        r1f = al.sb("d4_r1f", [128, 2, 4, D], F32)
        facc = al.sb("d4_facc", [128, 2, 4, D], F32)
        lncols = al.sb("d4_lnc", [128, 2, 4, 4], F32)
        tmpe = None
        csts = al.sb("d4_csts", [128, 6], F32)
        m4r = al.sb("d4_m4r", [1, 16], F32)
        sc1 = al.sb("d4_sc1", [1, 4], F32)
        trip = al.sb("d4_trip", [1, 1], F32)
        ptb = al.ps("d4_ptb", [128, 3072], BF16)
        po = al.ps("d4_po", [128, 4, 512], F32)
        scal_d = al.dram("d4_scal", [1, 2], F32)
        lw0 = al.sem("d4_lw0"); lw1 = al.sem("d4_lw1")
        lo0 = al.sem("d4_lo0"); lo1 = al.sem("d4_lo1")
        lr0 = al.sem("d4_lr0"); lr1 = al.sem("d4_lr1")
        sr0 = al.sem("d4_sr0"); sr1 = al.sem("d4_sr1")
        aux = al.sem("d4_aux")
        vself = al.sem("d4_vself"); sself = al.sem("d4_sself"); tself = al.sem("d4_tself")
        lw = [lw0, lw1]; lo = [lo0, lo1]; lr = [lr0, lr1]; sr = [sr0, sr1]
        register_const(nc, RC_MAGIC, csts[:, 2:3])
        register_const(nc, -RC_MAGIC, csts[:, 3:4])
        register_const(nc, 1.0 / 1024.0, csts[:, 4:5])
        register_const(nc, 1e-5, csts[:, 5:6])
        block = al.block()

        @block.gpsimd
        def _(gp):
            gp.dma_start(out=idb[:], in_=idn).then_inc(aux, 16)
            gp.dma_start(out=m4r[:], in_=m4s).then_inc(aux, 16)
            gp.wait_ge(vself, 5)
            gp.dma_start(out=scal_d, in_=sc1[0:1, 1:3]).then_inc(aux, 16)
            gp.wait_ge(aux, 48)
            gp.dma_start(out=csts[:, 0:2], in_=scal_d.broadcast_to((128, 2))).then_inc(aux, 16)
            for tb in range(NTB):
                for ic in range(1, NOC, 2):
                    idx = 24 * tb + ic
                    if idx >= 4:
                        pi = idx - 4
                        gp.wait_ge(vself, vq_m(pi // 24, pi % 24))
                    gp.dma_start(out=oin[:, idx % 4, :],
                                 in_=o_pre[128 * ic:128 * (ic + 1), 512 * tb:512 * (tb + 1)]).then_inc(lo[1], 16)
                if tb >= 2:
                    gp.wait_ge(vself, vt_m(tb - 2, 3))
                gp.dma_start(out=r1f[:, tb % 2, :, :],
                             in_=R1[512 * tb:512 * (tb + 1), :].rearrange("(t4 p) d -> p t4 d", p=128)).then_inc(lr[tb % 2], 16)
                if tb >= 1:
                    tbo = tb - 1
                    gp.wait_ge(vself, vt_m(tbo, 3))
                    if tbo >= 2:
                        gp.wait_ge(sr[tbo % 2], 16 * (tbo // 2))
                    gp.dma_start(out=R2_out[512 * tbo:512 * (tbo + 1), :].rearrange("(t4 p) d -> p t4 d", p=128),
                                 in_=facc[:, tbo % 2, :, :]).then_inc(sr[tbo % 2], 16)
            tbo = NTB - 1
            gp.wait_ge(vself, vt_m(tbo, 3))
            gp.wait_ge(sr[tbo % 2], 16 * (tbo // 2))
            gp.dma_start(out=R2_out[512 * tbo:512 * (tbo + 1), :].rearrange("(t4 p) d -> p t4 d", p=128),
                         in_=facc[:, tbo % 2, :, :]).then_inc(sr[tbo % 2], 16)
            gp.wait_ge(sr0, 16 * 4)
            gp.wait_ge(sr1, 16 * 4)

        @block.sync
        def _(sync):
            for hc in range(NC):
                if hc >= 2:
                    sync.wait_ge(vself, vdq_m(hc - 2))
                sync.dma_start(out=wdst[:, hc % 2, :], in_=w_down[128 * hc:128 * (hc + 1), :]).then_inc(lw[hc % 2], 16)
            for tb in range(NTB):
                for ic in range(NOC):
                    idx = 24 * tb + ic
                    if idx % 2 == 1:
                        continue
                    if idx >= 4:
                        pi = idx - 4
                        sync.wait_ge(vself, vq_m(pi // 24, pi % 24))
                    sync.dma_start(out=oin[:, idx % 4, :],
                                   in_=o_pre[128 * ic:128 * (ic + 1), 512 * tb:512 * (tb + 1)]).then_inc(lo[0], 16)

        @block.vector
        def _(vector):
            V = ESeq(vector, vself)
            vector.wait_ge(aux, 32)
            V.d(vector.tensor_reduce(sc1[0:1, 0:1], m4r[0:1, 0:8], AX.X, ALU.max))
            V.d(vector.tensor_scalar(sc1[0:1, 1:2], sc1[0:1, 0:1], 1.0 / 32767.0, None, ALU.mult))
            V.d(vector.tensor_scalar(sc1[0:1, 1:2], sc1[0:1, 1:2], 1e-12, None, ALU.max))
            V.d(vector.reciprocal(sc1[0:1, 2:3], sc1[0:1, 1:2]))
            V.d(vector.tensor_mul(sc1[0:1, 1:2], sc1[0:1, 1:2], m4r[0:1, 8:9]))
            V.d(vector.memset(trip[:], 0.0))
            V.d(vector.tensor_copy(trip[:].bitcast(I32), trip[:]))
            V.d(vector.memset(csts[:, 2:3], RC_MAGIC))
            V.d(vector.memset(csts[:, 3:4], -RC_MAGIC))
            V.d(vector.memset(csts[:, 4:5], 1.0 / 1024.0))
            V.d(vector.memset(csts[:, 5:6], 1e-5))
            assert V.n == 11
            for hc in range(NC):
                k2 = hc % 2
                vector.wait_ge(lw[k2], 16 * (hc // 2 + 1))
                if hc >= 2:
                    vector.wait_ge(tself, tp_m(hc - 2))
                w3 = wdst[:, k2, :].rearrange("p (b k) -> p b k", k=32)
                wq3 = wdqb[:, k2, :].rearrange("p (b k) -> p b k", k=32)
                V.d(vector.tensor_reduce(bq[:, k2, 0], w3, AX.X, ALU.max, apply_absolute_value=True))
                V.d(vector.tensor_scalar(bq[:, k2, 1], bq[:, k2, 0], 1.0 / 7.0, None, ALU.mult))
                V.d(vector.tensor_scalar(bq[:, k2, 1], bq[:, k2, 1], 1e-12, None, ALU.max))
                V.d(vector.reciprocal(bq[:, k2, 2], bq[:, k2, 1]))
                V.d(vector.tensor_tensor(w3, w3, bq[:, k2, 2].broadcast_to((128, 96, 32)), ALU.mult))
                V.d(vector.tensor_scalar(wdst[:, k2, :], wdst[:, k2, :], RC_MAGIC, None, ALU.add))
                V.d(vector.tensor_scalar(wdst[:, k2, :], wdst[:, k2, :], -RC_MAGIC, None, ALU.add))
                V.d(vector.tensor_tensor(wq3, w3, bq[:, k2, 1].broadcast_to((128, 96, 32)), ALU.mult))
                assert V.n == vdq_m(hc)
            assert V.n == VP4
            vector.wait_ge(aux, 64)
            for it in range(NTB + 1):
                if it < NTB:
                    tb = it
                    for ic in range(NOC):
                        idx = 24 * tb + ic
                        vector.wait_ge(sself, sq_m(tb, ic))
                        if tb >= 2 and ic == 0:
                            vector.wait_ge(tself, tm_m(tb - 2, 1, 3))
                        V.d(vector.tensor_scalar(ob[:, tb % 2, ic, :], oin[:, idx % 4, :], csts[:, 0:1], None, ALU.mult))
                        assert V.n == vq_m(tb, ic)
                if it >= 1:
                    tb = it - 1
                    for t4 in range(4):
                        vector.wait_ge(sself, sevac_m(tb, 1, t4))
                        vector.wait_ge(lr[tb % 2], 16 * (tb // 2 + 1))
                        fa = facc[:, tb % 2, t4, :]
                        V.d(vector.tensor_add(fa, fa, r1f[:, tb % 2, t4, :]))
                        V.d(vector.tensor_reduce(lncols[:, tb % 2, t4, 0:1], fa, AX.X, ALU.add, negate=True))
                        V.d(vector.tensor_scalar(lncols[:, tb % 2, t4, 0:1], lncols[:, tb % 2, t4, 0:1], 1.0 / D, None, ALU.mult))
                        V.d(vector.tensor_scalar(fa, fa, lncols[:, tb % 2, t4, 0:1], None, ALU.add))
                        vector.wait_ge(sself, sact_m(tb, t4))
                        V.d(vector.tensor_scalar(fa, fa, lncols[:, tb % 2, t4, 3:4], None, ALU.mult))
                        assert V.n == vt_m(tb, t4)

        @block.tensor
        def _(tensor):
            T = ESeq(tensor, tself, relax=True)
            tensor.wait_ge(aux, 32)
            for hc in range(NC):
                base_m = 11 if hc == 0 else vdq_m(hc - 1)
                for s in range(7):
                    tensor.wait_ge(vself, min(base_m + s + 1, VP4))
                    for _ in range(16):
                        tensor.ldweights(csts[:, 2:6].bitcast(BF16))
                k2 = hc % 2
                tensor.wait_ge(vself, vdq_m(hc))
                tensor.wait_ge(sself, hc)
                for ic in range(NOC):
                    ins = tensor.transpose(ptb[:, 128 * ic:128 * (ic + 1)], wdqb[:, k2, 128 * ic:128 * (ic + 1)], idb[:])
                T.d(ins)
                assert T.n == tp_m(hc)
            for tb in range(NTB):
                for h2 in range(2):
                    for tt4 in range(4):
                        tensor.wait_ge(vself, vqall(tb))
                        tensor.wait_ge(sself, SP4)
                        prev = (tb, h2 - 1, tt4) if h2 >= 1 else (tb - 1, 1, tt4)
                        if prev[0] >= 0:
                            tensor.wait_ge(sself, sevac_m(*prev))
                        for ic in range(NOC):
                            ins = tensor.matmul(po[:, tt4, :], ob[:, tb % 2, ic, 128 * tt4:128 * (tt4 + 1)],
                                                wdT[:, ic, 512 * h2:512 * (h2 + 1)], start=(ic == 0), stop=(ic == NOC - 1))
                        T.d(ins)
                        assert T.n == tm_m(tb, h2, tt4)

        @block.scalar
        def _(scalar):
            S = ESeq(scalar, sself)
            for hc in range(NC):
                scalar.wait_ge(tself, tp_m(hc))
                S.d(scalar.copy(wdT[:, :, 128 * hc:128 * (hc + 1)], ptb[:].rearrange("p (ic h) -> p ic h", h=128)))
                assert S.n == hc + 1
            def _qdq(tb, ic):
                idx = 24 * tb + ic
                scalar.wait_ge(lo[idx % 2], 32 * ((idx // 2 + 2) // 2))
                scalar.wait_ge(aux, 64)
                S.d(scalar.activation(out=oin[:, idx % 4, :], in_=oin[:, idx % 4, :], func=AF.Copy, scale=csts[:, 1:2], bias=RC_MAGIC))
                S.d(scalar.activation(out=oin[:, idx % 4, :], in_=oin[:, idx % 4, :], func=AF.Copy, bias=-RC_MAGIC))
                assert S.n == sq_m(tb, ic)

            def _evac(e, h2, tt4):
                scalar.wait_ge(tself, tm_m(e, h2, tt4))
                if e >= 2:
                    scalar.wait_ge(vself, vt_m(e - 2, 3))
                    scalar.wait_ge(sr[e % 2], 16 * (e // 2))
                S.d(scalar.copy(facc[:, e % 2, tt4, 512 * h2:512 * (h2 + 1)], po[:, tt4, :]))
                assert S.n == sevac_m(e, h2, tt4)

            def _ln(e, t4):
                scalar.wait_ge(vself, vt_m(e, t4, 4))
                S.d(scalar.activation(out=wdst[:, 0, 0:D], in_=facc[:, e % 2, t4, :], func=AF.Square, accum_out=lncols[:, e % 2, t4, 1:2]))
                S.d(scalar.activation(out=lncols[:, e % 2, t4, 3:4], in_=lncols[:, e % 2, t4, 1:2], func=AF.Ln, scale=1.0 / 1024.0, bias=1e-5))
                S.d(scalar.activation(out=lncols[:, e % 2, t4, 3:4], in_=lncols[:, e % 2, t4, 3:4], func=AF.Exp, scale=-0.5))
                assert S.n == sact_m(e, t4)

            for tb in range(NTB):
                for g in range(8):
                    for ic in range(3 * g, 3 * g + 3):
                        _qdq(tb, ic)
                    if tb >= 1:
                        _evac(tb - 1, g // 4, g % 4)
                if tb >= 1:
                    for t4 in range(4):
                        _ln(tb - 1, t4)
            for g in range(8):
                _evac(NTB - 1, g // 4, g % 4)
            for t4 in range(4):
                _ln(NTB - 1, t4)
    return nc


def build_stageB4():
    nc = bass.Bass("TRN2", target_bir_lowering=False)
    o_pre = nc.dram_tensor("o_pre", [3072, TT], F32, kind="ExternalInput").ap()
    m4s = nc.dram_tensor("m4s", [1, 16], F32, kind="ExternalInput").ap()
    R1 = nc.dram_tensor("R1", [TT, D], F32, kind="ExternalInput").ap()
    w_down = nc.dram_tensor("w_down", [D, 3072], F32, kind="ExternalInput").ap()
    idn = nc.dram_tensor("idn", [128, 128], F32, kind="ExternalInput").ap()
    R2 = nc.dram_tensor("R2", [TT, D], F32, kind="ExternalOutput").ap()
    emit_stageB4(nc, o_pre, m4s, R1, w_down, idn, R2)
    return nc



import math
TAU_INV = 1.0 / math.sqrt(128.0)
NW = 4736
NPC = 10


def emit_stageC1(nc, R2, g_ln, W_q, W_k, W_v, W_g, W_r, xl_u, xl_v, idn, catT,
                 QuT_d, QvT_d, KT_d, Tbl_d, V_d, G_d, fuse=None):
    pc_w = lambda pc: min(512, NW - 512 * pc)
    VPC = 5
    vtb_m = lambda pc: VPC + pc + 1
    VTB = VPC + NPC
    vx_m = lambda i: VTB + 3 * i + 3
    vq_m = lambda blk: VTB + 3 * NT + 2 * blk + 2
    VGB = VTB + 3 * NT + 16
    vv_m = lambda vb, i: VGB + 32 * vb + i + 1
    VGG = VGB + 128
    vg_m = lambda vb, i: VGG + 2 * (32 * vb + i) + 2
    sx_m = lambda i: 3 * i + 3
    sk_m = lambda blk: 3 * NT + blk + 1
    sg_m = lambda vb, i: 3 * NT + 8 + 32 * vb + i + 1
    tb_m = lambda pc: pc + 1
    tx_m = lambda i: NPC + i + 1
    tq_m = lambda blk: NPC + NT + 2 * blk + 2
    TQE = NPC + NT + 16
    tv_m = lambda vb, i: TQE + 32 * vb + i + 1
    TVE = TQE + 128
    tg_m = lambda vb, i: TVE + 32 * vb + i + 1
    with contextlib.ExitStack() as _stk:
        al = Alloc(nc, _stk, **(fuse or {}))
        idb = al.sb("e_idb", [128, 128], BF16)
        big = al.sb("e_big", [128, NC, NW], BF16)
        tbl = al.sb("e_tbl", [128, NW], BF16)
        wrb = al.sb("e_wrb", [128, NC, 128], BF16)
        wqb = al.sb("e_wqb", [128, NC, 128], BF16)
        wkb = al.sb("e_wkb", [128, NC, 128], BF16)
        wvb = al.sb("e_wvb", [128, 2, NC, 512], BF16)
        glc = al.sb("e_glc", [128, NC, 1], F32)
        xlq = al.sb("e_xlq", [128, 2], F32)
        rf = al.sb("e_rf", [128, 2, D], F32)
        xb0 = al.sb("e_xb0", [128, 2, D], BF16)
        quo = al.sb("e_quo", [128, 2, 512], BF16)
        qvo = al.sb("e_qvo", [128, 2, 512], BF16)
        kto = al.sb("e_kto", [128, 2, 512], BF16)
        vbo = al.sb("e_vbo", [128, 2, 512], BF16)
        gbo = al.sb("e_gbo", [128, 2, 512], BF16)
        gh = al.sb("e_gh", [128, 2, 512], F32)
        thb = al.sb("e_thb", [128, 2, 512], F32)
        sscol = al.sb("e_ssc", [128, 2, 2], F32)
        tmpe = al.sb("e_tmpe", [128, D], F32)
        zpad = al.sb("e_zpad", [128, 2048], BF16)
        csts = al.sb("e_csts", [128, 2], F32)
        trip = al.sb("e_trip", [1, 1], F32)
        ptb = al.ps("e_ptb", [128, 2, 512], F32)
        ptr = al.ps("e_ptr", [128, 2, D], BF16)
        pv = al.ps("e_pv", [128, 2, 512], F32)
        lw0 = al.sem("e_lw0"); lw1 = al.sem("e_lw1")
        lwp = [lw0, lw1]
        ld0 = al.sem("e_ld0"); ld1 = al.sem("e_ld1")
        aux = al.sem("e_aux"); auxs = al.sem("e_auxs")
        s_q0 = al.sem("e_s_q0"); s_q1 = al.sem("e_s_q1")
        s_v0 = al.sem("e_s_v0"); s_v1 = al.sem("e_s_v1")
        vself = al.sem("e_vself"); sself = al.sem("e_sself"); tself = al.sem("e_tself")
        ld = [ld0, ld1]
        s_q = [s_q0, s_q1]; s_v = [s_v0, s_v1]
        register_const(nc, 1.0 / 1024.0, csts[:, 0:1])
        register_const(nc, 1e-6, csts[:, 1:2])
        block = al.block()

        @block.gpsimd
        def _(gp):
            gp.dma_start(out=idb[:], in_=idn).then_inc(aux, 16)
            gp.dma_start(out=wrb[:], in_=W_r.rearrange("(c p) f -> p c f", p=128)).then_inc(aux, 16)
            gp.dma_start(out=wqb[:], in_=W_q.rearrange("(c p) f -> p c f", p=128)).then_inc(aux, 16)
            gp.dma_start(out=wkb[:], in_=W_k.rearrange("(c p) f -> p c f", p=128)).then_inc(aux, 16)
            gp.dma_start(out=glc[:, :, 0], in_=g_ln.rearrange("(c p) -> p c", p=128), allow_slow_non_contiguous=True).then_inc(aux, 16)
            gp.dma_start(out=xlq[:, 0:1], in_=xl_u.rearrange("(k one) -> k one", one=1)).then_inc(aux, 16)
            gp.dma_start(out=xlq[:, 1:2], in_=xl_v.rearrange("(k one) -> k one", one=1)).then_inc(aux, 16)
            gp.dma_start(out=big[:], in_=catT).then_inc(aux, 16)
            for i in range(NT):
                if i >= 2:
                    gp.wait_ge(vself, vx_m(i - 2))
                gp.dma_start(out=rf[:, i % 2, :], in_=R2[128 * i:128 * (i + 1), :]).then_inc(ld[i % 2], 16)
            gp.wait_ge(vself, VPC)
            gp.dma_start(out=KT_d[:, 4096:4352], in_=zpad[:, 0:256]).then_inc(aux, 16)
            gp.dma_start(out=V_d[4096:4224, :], in_=zpad[:, :]).then_inc(aux, 16)
            for m in range(2):
                WX = W_v if m == 0 else W_g
                for vb in range(4):
                    j = 4 * m + vb
                    if j >= 2:
                        pj = j - 2
                        pm, pvb = divmod(pj, 4)
                        gp.wait_ge(tself, (tv_m(pvb, NT - 1) if pm == 0 else tg_m(pvb, NT - 1)))
                    gp.dma_start(out=wvb[:, j % 2, :, :],
                                 in_=WX.rearrange("(c p) v -> p c v", p=128)[:, :, 512 * vb:512 * (vb + 1)]).then_inc(lwp[j % 2], 16)
            gp.wait_ge(aux, 16 * 10)

        @block.sync
        def _(sync):
            sync.wait_ge(vself, VTB)
            sync.dma_start(out=Tbl_d, in_=tbl[:]).then_inc(auxs, 16)
            for blk in range(8):
                k2 = blk % 2
                sync.wait_ge(vself, vq_m(blk))
                if blk >= 2:
                    sync.wait_ge(s_q[k2], 16 * 3 * (blk // 2))
                sync.dma_start(out=QuT_d[:, 512 * blk:512 * (blk + 1)], in_=quo[:, k2, :]).then_inc(s_q[k2], 16)
                sync.dma_start(out=QvT_d[:, 512 * blk:512 * (blk + 1)], in_=qvo[:, k2, :]).then_inc(s_q[k2], 16)
                sync.wait_ge(sself, sk_m(blk))
                sync.dma_start(out=KT_d[:, 512 * blk:512 * (blk + 1)], in_=kto[:, k2, :]).then_inc(s_q[k2], 16)
            for vb in range(4):
                for i in range(NT):
                    idx = 32 * vb + i
                    sync.wait_ge(vself, vv_m(vb, i))
                    if idx >= 2:
                        sync.wait_ge(s_v[idx % 2], 16 * (idx // 2))
                    sync.dma_start(out=V_d[128 * i:128 * (i + 1), 512 * vb:512 * (vb + 1)],
                                   in_=vbo[:, idx % 2, :]).then_inc(s_v[idx % 2], 16)
            for vb in range(4):
                for i in range(NT):
                    idx = 32 * vb + i
                    sync.wait_ge(vself, vg_m(vb, i))
                    sync.wait_ge(s_v[idx % 2], 16 * (64 + idx // 2))
                    sync.dma_start(out=G_d[128 * i:128 * (i + 1), 512 * vb:512 * (vb + 1)],
                                   in_=gbo[:, idx % 2, :]).then_inc(s_v[idx % 2], 16)
            sync.wait_ge(s_v0, 16 * 128)
            sync.wait_ge(s_v1, 16 * 128)
            sync.wait_ge(s_q0, 16 * 12)
            sync.wait_ge(s_q1, 16 * 12)
            sync.wait_ge(auxs, 16)

        @block.vector
        def _(vector):
            V = ESeq(vector, vself)
            V.d(vector.memset(trip[:], 0.0))
            V.d(vector.tensor_copy(trip[:].bitcast(I32), trip[:]))
            V.d(vector.memset(zpad[:], 0.0))
            V.d(vector.memset(csts[:, 0:1], 1.0 / 1024.0))
            V.d(vector.memset(csts[:, 1:2], 1e-6))
            assert V.n == VPC
            for pc in range(NPC):
                w = pc_w(pc)
                vector.wait_ge(tself, tb_m(pc))
                V.d(vector.tensor_copy(tbl[:, 512 * pc:512 * pc + w], ptb[:, pc % 2, 0:w]))
                assert V.n == vtb_m(pc)
            for i in range(NT):
                k2 = i % 2
                vector.wait_ge(ld[k2], 16 * (i // 2 + 1))
                vector.wait_ge(sself, sx_m(i))
                if i >= 2:
                    vector.wait_ge(tself, tx_m(i - 2))
                V.d(vector.tensor_scalar(rf[:, k2, :], rf[:, k2, :], sscol[:, k2, 1:2], None, ALU.mult))
                V.d(vector.tensor_copy(xb0[:, k2, :], rf[:, k2, :]))
                vector.wait_ge(tself, tx_m(i))
                V.d(vector.tensor_tensor(big[:, :, 128 * i:128 * (i + 1)],
                                         ptr[:, k2, :].rearrange("p (c t) -> p c t", t=128),
                                         glc[:].broadcast_to((128, NC, 128)), ALU.mult))
                assert V.n == vx_m(i)
            for blk in range(8):
                k2 = blk % 2
                vector.wait_ge(tself, tq_m(blk) - 1)
                if blk >= 2:
                    vector.wait_ge(s_q[k2], 16 * 3 * (blk // 2))
                V.d(vector.tensor_scalar(quo[:, k2, :], ptb[:, k2, :], TAU_INV, xlq[:, 0:1], ALU.mult, ALU.add))
                V.d(vector.tensor_scalar(qvo[:, k2, :], ptb[:, k2, :], TAU_INV, xlq[:, 1:2], ALU.mult, ALU.add))
                assert V.n == vq_m(blk)
            for vb in range(4):
                for i in range(NT):
                    idx = 32 * vb + i
                    k2 = idx % 2
                    vector.wait_ge(tself, tv_m(vb, i))
                    if idx >= 2:
                        vector.wait_ge(s_v[k2], 16 * (idx // 2))
                    V.d(vector.tensor_copy(vbo[:, k2, :], pv[:, k2, :]))
                    assert V.n == vv_m(vb, i)
            for vb in range(4):
                for i in range(NT):
                    idx = 32 * vb + i
                    k2 = idx % 2
                    vector.wait_ge(tself, tg_m(vb, i))
                    if idx >= 2:
                        vector.wait_ge(s_v[k2], 16 * (64 + idx // 2))
                    V.d(vector.tensor_scalar(gh[:, k2, :], pv[:, k2, :], 0.5, None, ALU.mult))
                    vector.wait_ge(sself, sg_m(vb, i))
                    V.d(vector.scalar_tensor_tensor(gbo[:, k2, :], thb[:, k2, :], 1.0, gh[:, k2, :], ALU.add, ALU.mult))
                    assert V.n == vg_m(vb, i)

        @block.tensor
        def _(tensor):
            T = ESeq(tensor, tself, relax=True)
            tensor.wait_ge(aux, 16 * 8)
            for pc in range(NPC):
                w = pc_w(pc)
                if pc >= 2:
                    tensor.wait_ge(vself, vtb_m(pc - 2))
                for c in range(NC):
                    ins = tensor.matmul(ptb[:, pc % 2, 0:w], wrb[:, c, :],
                                        big[:, c, 512 * pc:512 * pc + w], start=(c == 0), stop=(c == NC - 1))
                T.d(ins)
                assert T.n == tb_m(pc)
            for i in range(NT):
                k2 = i % 2
                vector_mark = vx_m(i) - 1
                tensor.wait_ge(vself, vector_mark)
                if i >= 2:
                    tensor.wait_ge(vself, vx_m(i - 2))
                for c in range(NC):
                    ins = tensor.transpose(ptr[:, k2, 128 * c:128 * (c + 1)], xb0[:, k2, 128 * c:128 * (c + 1)], idb[:])
                T.d(ins)
                assert T.n == tx_m(i)
            for blk in range(8):
                k2 = blk % 2
                tensor.wait_ge(vself, vx_m(NT - 1))
                if blk >= 2:
                    tensor.wait_ge(vself, vq_m(blk - 2))
                    tensor.wait_ge(sself, sk_m(blk - 2))
                for c in range(NC):
                    ins = tensor.matmul(ptb[:, k2, :], wqb[:, c, :], big[:, c, 512 * blk:512 * (blk + 1)],
                                        start=(c == 0), stop=(c == NC - 1))
                T.d(ins)
                for c in range(NC):
                    ins = tensor.matmul(pv[:, k2, :], wkb[:, c, :], big[:, c, 512 * blk:512 * (blk + 1)],
                                        start=(c == 0), stop=(c == NC - 1))
                T.d(ins)
                assert T.n == tq_m(blk)
            for m in range(2):
                for vb in range(4):
                    j = 4 * m + vb
                    tensor.wait_ge(lwp[j % 2], 16 * (j // 2 + 1))
                    for i in range(NT):
                        idx = 32 * vb + i
                        k2 = idx % 2
                        if idx >= 2:
                            if m == 0:
                                tensor.wait_ge(vself, vv_m((idx - 2) // 32, (idx - 2) % 32))
                            else:
                                tensor.wait_ge(vself, vg_m((idx - 2) // 32, (idx - 2) % 32))
                        elif m == 0:
                            tensor.wait_ge(sself, sk_m(6 + idx))
                        else:
                            tensor.wait_ge(vself, vv_m(3, 30 + idx))
                        for c in range(NC):
                            ins = tensor.matmul(pv[:, k2, :], big[:, c, 128 * i:128 * (i + 1)],
                                                wvb[:, j % 2, c, :], start=(c == 0), stop=(c == NC - 1))
                        T.d(ins)
                        assert T.n == (tv_m(vb, i) if m == 0 else tg_m(vb, i))

        @block.scalar
        def _(scalar):
            S = ESeq(scalar, sself)
            for i in range(NT):
                k2 = i % 2
                scalar.wait_ge(ld[k2], 16 * (i // 2 + 1))
                scalar.wait_ge(vself, VPC)
                S.d(scalar.activation(out=tmpe[:], in_=rf[:, k2, :], func=AF.Square, accum_out=sscol[:, k2, 0:1]))
                S.d(scalar.activation(out=sscol[:, k2, 1:2], in_=sscol[:, k2, 0:1], func=AF.Ln,
                                      scale=csts[:, 0:1], bias=csts[:, 1:2]))
                S.d(scalar.activation(out=sscol[:, k2, 1:2], in_=sscol[:, k2, 1:2], func=AF.Exp, scale=-0.5))
                assert S.n == sx_m(i)
            for blk in range(8):
                k2 = blk % 2
                scalar.wait_ge(tself, tq_m(blk))
                if blk >= 2:
                    scalar.wait_ge(s_q[k2], 16 * 3 * (blk // 2))
                S.d(scalar.copy(kto[:, k2, :], pv[:, k2, :]))
                assert S.n == sk_m(blk)
            for vb in range(4):
                for i in range(NT):
                    idx = 32 * vb + i
                    k2 = idx % 2
                    scalar.wait_ge(vself, vg_m(vb, i) - 1)
                    S.d(scalar.activation(out=thb[:, k2, :], in_=gh[:, k2, :], func=AF.Tanh))
                    assert S.n == sg_m(vb, i)
    return nc


def build_stageC1():
    nc = bass.Bass("TRN2", target_bir_lowering=False)
    R2 = nc.dram_tensor("R2", [TT, D], F32, kind="ExternalInput").ap()
    g_ln = nc.dram_tensor("g_ln", [D], F32, kind="ExternalInput").ap()
    W_q = nc.dram_tensor("W_q", [D, 128], F32, kind="ExternalInput").ap()
    W_k = nc.dram_tensor("W_k", [D, 128], F32, kind="ExternalInput").ap()
    W_v = nc.dram_tensor("W_v", [D, 2048], F32, kind="ExternalInput").ap()
    W_g = nc.dram_tensor("W_g", [D, 2048], F32, kind="ExternalInput").ap()
    W_r = nc.dram_tensor("W_r", [D, 128], F32, kind="ExternalInput").ap()
    xl_u = nc.dram_tensor("xl_u", [128], F32, kind="ExternalInput").ap()
    xl_v = nc.dram_tensor("xl_v", [128], F32, kind="ExternalInput").ap()
    idn = nc.dram_tensor("idn", [128, 128], F32, kind="ExternalInput").ap()
    catT = nc.dram_tensor("catT", [128, NC, NW], BF16, kind="ExternalInput").ap()
    QuT = nc.dram_tensor("QuT", [128, TT], BF16, kind="ExternalOutput").ap()
    QvT = nc.dram_tensor("QvT", [128, TT], BF16, kind="ExternalOutput").ap()
    KT = nc.dram_tensor("KT", [128, 4352], BF16, kind="ExternalOutput").ap()
    Tbl = nc.dram_tensor("Tbl", [128, NW], BF16, kind="ExternalOutput").ap()
    V_d = nc.dram_tensor("V_d", [4224, 2048], BF16, kind="ExternalOutput").ap()
    G_d = nc.dram_tensor("G_d", [TT, 2048], BF16, kind="ExternalOutput").ap()
    emit_stageC1(nc, R2, g_ln, W_q, W_k, W_v, W_g, W_r, xl_u, xl_v, idn, catT,
                 QuT, QvT, KT, Tbl, V_d, G_d)
    return nc


def host_catT():
    m = np.arange(NW)
    d = 4223 - m
    i = np.arange(512, dtype=np.float64)
    inv_lams = 1.0 / (10000.0 ** (2.0 * i / 1024.0))
    pre = d[:, None].astype(np.float64) * inv_lams[None, :]
    cat = np.concatenate([np.sin(pre), np.cos(pre)], axis=1)
    cat[(d < 0) | (d > 4095), :] = 0.0
    catT = cat.T.reshape(NC, 128, NW).transpose(1, 0, 2)
    return np.ascontiguousarray(catT.astype(ml_dtypes.bfloat16))



def _c2_geom(qi):
    nkj = -(-(qi + 1) * 128 // 384)
    nkc = qi + 1
    npt = -(-nkc // 4)
    parts = [kj for kj in range(nkj) if 128 * qi - 384 * kj < 383]
    return nkj, nkc, npt, parts


def emit_stageC2(nc, R2, QuT_d, QvT_d, KT_d, Tbl_d, V_d, G_d, W_o, idn, R3_out, fuse=None):
    G2 = _c2_geom
    PAIRS = []
    for qi in range(NT):
        PAIRS += [(qi, kj) for kj in range(G2(qi)[0])]
    GPAIR = {pr: i for i, pr in enumerate(PAIRS)}
    FC4, SLOT = {}, {}
    c4 = [0, 0, 0, 0]
    for i, pr in enumerate(PAIRS):
        SLOT[pr] = i % 4
        c4[i % 4] += 1
        FC4[pr] = c4[i % 4]
    SIDX = {}
    g = 0
    for qi in range(NT):
        for sj in range(-(-G2(qi)[1] // 2)):
            SIDX[(qi, sj)] = g; g += 1
    ISIDX = {v: k for k, v in SIDX.items()}

    def row_geom(q):
        L = 384 * G2(q)[0]
        return L, -(-L // 1024)

    def pair_items(pr):
        qi, kj = pr
        out = [("ac", qi, kj), ("sr", qi, kj), ("bd", qi, kj), ("gb", qi, kj),
               ("fwt", qi, kj), ("facm", qi, kj)]
        if kj in G2(qi)[3]:
            out.append(("msk", qi, kj))
        return out

    def p2_stream(q):
        nkj, nkc, npt, parts = G2(q)
        s = []
        for g4 in range(npt):
            s += [("ptr", q, g4), ("pte", q, g4)]
        s += [("expS", q, 0), ("vsum", q, 0)]
        for kc in range(nkc):
            s.append(("okc", q, kc))
        for c in range(4):
            s += [("obf", q, c), ("wvg", q, c)]
        for g4 in range(4):
            s += [("wvt", q, g4), ("wve", q, g4)]
        s += [("wo", q, 0), ("wo", q, 1)]
        return s

    def epi_stream(q):
        return [("fac", q, 0), ("vln1", q, 0), ("sact", q, 0), ("vln2", q, 0)]

    def prev_pte(q, par):
        for row in range(q - 1, -1, -1):
            npt1 = G2(row)[2]
            g4p = npt1 - 1 - ((npt1 - 1 - par) % 2)
            if g4p >= 0:
                return ("pte", row, g4p)
        return None

    SCHED = []
    for it in range(NT + 2):
        pairs = [pair_items((it, kj)) for kj in range(G2(it)[0])] if it < NT else []
        p2 = p2_stream(it - 1) if 1 <= it <= NT else []
        if 0 <= it - 2 <= NT - 1:
            p2 = epi_stream(it - 2) + p2
        for pl in pairs[:2]:
            SCHED += pl
        rem = pairs[2:]
        npp, np2 = len(rem), len(p2)
        pi = 0
        for idx, item in enumerate(p2):
            SCHED.append(item)
            want = ((idx + 1) * npp) // np2 if np2 else npp
            while pi < want:
                SCHED += rem[pi]; pi += 1
        while pi < npp:
            SCHED += rem[pi]; pi += 1

    TK = {"ac", "bd", "ptr", "okc", "wvt", "wo"}
    SK1 = {"sr", "expS", "pte", "wve", "fac"}
    VK1 = {"obf", "wvg", "gb", "vsum"}
    GK1 = {"msk"}
    TM, SM, VM, GM = {}, {}, {}, {}
    tn, sn, vn, gn = 0, 0, 3, 0
    for item in SCHED:
        k = item[0]
        if k in TK:
            tn += 1; TM[item] = tn
        elif k in SK1:
            sn += 1; SM[item] = sn
        elif k in GK1:
            gn += 1; GM[item] = gn
        elif k == "sact":
            sn += 3; SM[item] = sn
        elif k in VK1:
            vn += 1; VM[item] = vn
        elif k == "vln1":
            vn += 4; VM[item] = vn
        elif k == "vln2":
            vn += 1; VM[item] = vn

    CNT4END = {}
    c4e = [0, 0, 0, 0]
    for pr in PAIRS:
        c4e[SLOT[pr]] = FC4[pr]
        CNT4END[pr[0]] = list(c4e)
    MSKLAST = {}
    for item in SCHED:
        if item[0] == "msk":
            MSKLAST[item[1]] = GM[item]

    def win_waits(eng, q, lo, hi):
        parts = G2(q)[3]
        gmax = 0
        for kj in range(lo // 384, min((hi - 1) // 384, G2(q)[0] - 1) + 1):
            eng.wait_ge(fr[SLOT[(q, kj)]], 16 * FC4[(q, kj)])
            if kj in parts:
                gmax = max(gmax, GM[("msk", q, kj)])
        if gmax:
            eng.wait_ge(gself, gmax)

    with contextlib.ExitStack() as _stk:
        al = Alloc(nc, _stk, **(fuse or {}))
        idb = al.sb("f_idb", [128, 128], BF16)
        idbf = al.sb("f_idbf", [128, 128], F32)
        qut = al.sb("f_qut", [128, TT], BF16)
        qvt = al.sb("f_qvt", [128, TT], BF16)
        ktb = al.sb("f_ktb", [128, 4352], BF16)
        tbl = al.sb("f_tbl", [128, NW], BF16)
        wob = al.sb("f_wob", [128, 16, D], BF16)
        srow = al.sb("f_srow", [128, 2, 4352], F32)
        pts = al.sb("f_pts", [128, 4352], BF16)
        gsil = al.sb("f_gsil", [128, 2, 2048], BF16)
        vbuf = al.sb("f_vbuf", [128, 3, 2, 2048], BF16)
        gbuf = al.sb("f_gbuf", [128, 2, 512], F32)
        obf = al.sb("f_obf", [128, 2048], F32)
        wvg = al.sb("f_wvg", [128, 2048], F32)
        wvt = al.sb("f_wvt", [128, 2048], BF16)
        facc = al.sb("f_facc", [128, 2, D], F32)
        r2f = al.sb("f_r2f", [128, 2, D], F32)
        ncol = al.sb("f_ncol", [128, 2, 24], F32)
        lnc = al.sb("f_lnc", [128, 2, 4], F32)
        tmpe = al.sb("f_tmpe", [128, D], F32)
        csts = al.sb("f_csts", [128, 4], F32)
        pA = al.ps("f_pA", [128, 512], F32)
        pB = al.ps("f_pB", [128, 512], F32)
        pT = al.ps("f_pT", [128, 2, 512], F32)
        pO = al.ps("f_pO", [128, 2048], F32)
        flat = [al.dram(f"f_flat{k}", [128 * 513], F32) for k in range(4)]
        aux = al.sem("f_aux"); auxs = al.sem("f_auxs")
        lg0 = al.sem("f_lg0"); lg1 = al.sem("f_lg1")
        lv0 = al.sem("f_lv0"); lv1 = al.sem("f_lv1"); lv2 = al.sem("f_lv2")
        lr0 = al.sem("f_lr0"); lr1 = al.sem("f_lr1")
        fw0 = al.sem("f_fw0"); fw1 = al.sem("f_fw1")
        fw2 = al.sem("f_fw2"); fw3 = al.sem("f_fw3")
        fr0 = al.sem("f_fr0"); fr1 = al.sem("f_fr1")
        fr2 = al.sem("f_fr2"); fr3 = al.sem("f_fr3")
        sr0 = al.sem("f_sr0"); sr1 = al.sem("f_sr1")
        vself = al.sem("f_vself"); sself = al.sem("f_sself"); tself = al.sem("f_tself")
        gself = al.sem("f_gself")
        lg = [lg0, lg1]; lv = [lv0, lv1, lv2]; lr = [lr0, lr1]
        fw = [fw0, fw1, fw2, fw3]; fr = [fr0, fr1, fr2, fr3]; sr = [sr0, sr1]
        register_const(nc, 1.0 / 1024.0, csts[:, 0:1])
        register_const(nc, 1e-5, csts[:, 1:2])
        register_const(nc, -20.0, csts[:, 2:3])
        block = al.block()

        @block.gpsimd
        def _(gp):
            G = ESeq(gp, gself)
            gp.dma_start(out=idb[:], in_=idn).then_inc(aux, 16)
            gp.dma_start(out=idbf[:], in_=idn).then_inc(aux, 16)
            gp.dma_start(out=wob[:], in_=W_o.rearrange("(c p) h -> p c h", p=128)).then_inc(aux, 16)
            for item in SCHED:
                k = item[0]
                if k == "facm":
                    qi, kj = item[1], item[2]
                    s4 = SLOT[(qi, kj)]
                    gp.wait_ge(fw[s4], 16 * FC4[(qi, kj)])
                    gp.wait_ge(sself, SM[("sr", qi, kj)])
                    gp.dma_start(out=srow[:, qi % 2, 384 * kj:384 * (kj + 1)],
                                 in_=flat[s4][128:128 + 128 * 512].rearrange("(r c) -> r c", c=512)[:, 0:384],
                                 accum_op=ALU.add).then_inc(fr[s4], 16)
                elif k == "msk":
                    qi, kj = item[1], item[2]
                    s4 = SLOT[(qi, kj)]
                    rel = 128 * qi - 384 * kj
                    gp.wait_ge(fr[s4], 16 * FC4[(qi, kj)])
                    G.d(gp.affine_select(srow[:, qi % 2, 384 * kj:384 * (kj + 1)],
                                         srow[:, qi % 2, 384 * kj:384 * (kj + 1)],
                                         [[-1, 384]], ALU.is_ge, -1e30, base=rel, channel_multiplier=1))

        @block.sync
        def _(sync):
            sync.dma_start(out=qut[:], in_=QuT_d).then_inc(auxs, 16)
            sync.dma_start(out=qvt[:], in_=QvT_d).then_inc(auxs, 16)
            sync.dma_start(out=ktb[:], in_=KT_d).then_inc(auxs, 16)
            sync.dma_start(out=tbl[:], in_=Tbl_d).then_inc(auxs, 16)
            for q in range(NT):
                nkc = G2(q)[1]
                k2 = q % 2
                if q >= 2:
                    sync.wait_ge(vself, VM[("wvg", q - 2, 3)])
                sync.dma_start(out=gsil[:, k2, :], in_=G_d[128 * q:128 * (q + 1), :]).then_inc(lg[k2], 16)
                if q >= 2:
                    sync.wait_ge(vself, VM[("vln1", q - 2, 0)] - 3)
                sync.dma_start(out=r2f[:, k2, :], in_=R2[128 * q:128 * (q + 1), :]).then_inc(lr[k2], 16)
                for sj in range(-(-nkc // 2)):
                    gs = SIDX[(q, sj)]
                    p3 = gs % 3
                    if gs >= 3:
                        pqi, psj = ISIDX[gs - 3]
                        pkc = min(2 * psj + 1, G2(pqi)[1] - 1)
                        sync.wait_ge(tself, TM[("okc", pqi, pkc)])
                    sync.dma_start(out=vbuf[:, p3, :, :],
                                   in_=V_d[256 * sj:256 * (sj + 1), :].rearrange("(t p) v -> p t v", p=128)).then_inc(lv[p3], 16)
                if q >= 2:
                    qo = q - 2
                    sync.wait_ge(vself, VM[("vln2", qo, 0)])
                    if qo >= 2:
                        sync.wait_ge(sr[qo % 2], 16 * (qo // 2))
                    sync.dma_start(out=R3_out[128 * qo:128 * (qo + 1), :], in_=facc[:, qo % 2, :]).then_inc(sr[qo % 2], 16)
            for qo in (NT - 2, NT - 1):
                sync.wait_ge(vself, VM[("vln2", qo, 0)])
                sync.wait_ge(sr[qo % 2], 16 * (qo // 2))
                sync.dma_start(out=R3_out[128 * qo:128 * (qo + 1), :], in_=facc[:, qo % 2, :]).then_inc(sr[qo % 2], 16)
            sync.wait_ge(sr0, 16 * (NT // 2))
            sync.wait_ge(sr1, 16 * (NT // 2))

        @block.vector
        def _(vector):
            V = ESeq(vector, vself)
            V.d(vector.memset(csts[:, 0:1], 1.0 / 1024.0))
            V.d(vector.memset(csts[:, 1:2], 1e-5))
            V.d(vector.memset(csts[:, 2:3], -20.0))
            for item in SCHED:
                k = item[0]
                if k == "gb":
                    qi, kj = item[1], item[2]
                    gi = GPAIR[(qi, kj)]
                    vector.wait_ge(tself, TM[("bd", qi, kj)])
                    if gi >= 2:
                        ppr = PAIRS[gi - 2]
                        vector.wait_ge(fw[SLOT[ppr]], 16 * FC4[ppr])
                    V.d(vector.tensor_copy(gbuf[:, gi % 2, :], pB[:]))
                elif k == "vsum":
                    q = item[1]
                    q2 = q % 2
                    vector.wait_ge(sself, SM[("expS", q, 0)])
                    V.d(vector.reciprocal(ncol[:, q2, 1:2], ncol[:, q2, 2:3]))
                elif k == "obf":
                    q, c = item[1], item[2]
                    if c == 0:
                        vector.wait_ge(tself, TM[("okc", q, G2(q)[1] - 1)])
                    V.d(vector.tensor_scalar(obf[:, 512 * c:512 * (c + 1)], pO[:, 512 * c:512 * (c + 1)],
                                             ncol[:, q % 2, 1:2], None, ALU.mult))
                elif k == "wvg":
                    q, c = item[1], item[2]
                    if c == 0:
                        vector.wait_ge(lg[q % 2], 16 * (q // 2 + 1))
                    if q >= 1:
                        vector.wait_ge(tself, TM[("wvt", q - 1, c)])
                    V.d(vector.tensor_mul(wvg[:, 512 * c:512 * (c + 1)], obf[:, 512 * c:512 * (c + 1)],
                                          gsil[:, q % 2, 512 * c:512 * (c + 1)]))
                elif k == "vln1":
                    q = item[1]
                    q2 = q % 2
                    fa = facc[:, q2, :]
                    vector.wait_ge(sself, SM[("fac", q, 0)])
                    vector.wait_ge(lr[q2], 16 * (q // 2 + 1))
                    V.d(vector.tensor_add(fa, fa, r2f[:, q2, :]))
                    V.d(vector.tensor_reduce(lnc[:, q2, 0:1], fa, AX.X, ALU.add, negate=True))
                    V.d(vector.tensor_scalar(lnc[:, q2, 0:1], lnc[:, q2, 0:1], 1.0 / D, None, ALU.mult))
                    V.d(vector.tensor_scalar(fa, fa, lnc[:, q2, 0:1], None, ALU.add))
                elif k == "vln2":
                    q = item[1]
                    q2 = q % 2
                    vector.wait_ge(sself, SM[("sact", q, 0)])
                    V.d(vector.tensor_scalar(facc[:, q2, :], facc[:, q2, :], lnc[:, q2, 3:4], None, ALU.mult))

        @block.tensor
        def _(tensor):
            T = ESeq(tensor, tself, relax=True)
            tensor.wait_ge(auxs, 16 * 4)
            tensor.wait_ge(aux, 16 * 3)
            for item in SCHED:
                k = item[0]
                if k == "ac":
                    qi, kj = item[1], item[2]
                    gi = GPAIR[(qi, kj)]
                    if gi >= 1:
                        tensor.wait_ge(sself, SM[("sr",) + PAIRS[gi - 1]])
                    T.d(tensor.matmul(pA[:, 0:384], qut[:, 128 * qi:128 * (qi + 1)],
                                      ktb[:, 384 * kj:384 * (kj + 1)], start=True, stop=True))
                elif k == "bd":
                    qi, kj = item[1], item[2]
                    gi = GPAIR[(qi, kj)]
                    if gi >= 1:
                        tensor.wait_ge(vself, VM[("gb",) + PAIRS[gi - 1]])
                    rel = 128 * qi - 384 * kj
                    mlo = 4095 - rel
                    T.d(tensor.matmul(pB[:, :], qvt[:, 128 * qi:128 * (qi + 1)],
                                      tbl[:, mlo:mlo + 512], start=True, stop=True))
                elif k == "ptr":
                    q, g4 = item[1], item[2]
                    nkc, npt = G2(q)[1], G2(q)[2]
                    nch = min(4, nkc - 4 * g4)
                    win_waits(tensor, q, 512 * g4, 512 * g4 + 128 * nch)
                    if g4 >= 2:
                        tensor.wait_ge(sself, SM[("pte", q, g4 - 2)])
                    else:
                        pp = prev_pte(q, g4)
                        if pp is not None:
                            tensor.wait_ge(sself, SM[pp])
                    for cc in range(nch):
                        kc = 4 * g4 + cc
                        ins = tensor.transpose(pT[:, g4 % 2, 128 * cc:128 * (cc + 1)],
                                               srow[:, q % 2, 128 * kc:128 * (kc + 1)], idbf[:])
                    T.d(ins)
                elif k == "okc":
                    q, kc = item[1], item[2]
                    nkc = G2(q)[1]
                    gs = SIDX[(q, kc // 2)]
                    p3 = gs % 3
                    tensor.wait_ge(lv[p3], 16 * (gs // 3 + 1))
                    tensor.wait_ge(sself, SM[("pte", q, kc // 4)])
                    if kc == 0 and q >= 1:
                        tensor.wait_ge(sself, SM[("fac", q - 1, 0)])
                        tensor.wait_ge(vself, VM[("obf", q - 1, 3)])
                    for nn in range(4):
                        ins = tensor.matmul(pO[:, 512 * nn:512 * (nn + 1)], pts[:, 128 * kc:128 * (kc + 1)],
                                            vbuf[:, p3, kc % 2, 512 * nn:512 * (nn + 1)],
                                            start=(kc == 0), stop=(kc == nkc - 1))
                    T.d(ins)
                elif k == "wvt":
                    q, g4 = item[1], item[2]
                    tensor.wait_ge(vself, VM[("wvg", q, g4)])
                    tensor.wait_ge(vself, VM[("obf", q, 2 + g4 % 2)])
                    if g4 >= 2:
                        tensor.wait_ge(sself, SM[("wve", q, g4 - 2)])
                    base = 1024 + 512 * (g4 % 2)
                    for cc in range(4):
                        vc = 4 * g4 + cc
                        ins = tensor.transpose(pO[:, base + 128 * cc:base + 128 * (cc + 1)],
                                               wvg[:, 128 * vc:128 * (vc + 1)], idbf[:])
                    T.d(ins)
                elif k == "wo":
                    q, h = item[1], item[2]
                    tensor.wait_ge(sself, SM[("wve", q, 3)])
                    tensor.wait_ge(vself, VM[("obf", q, 3)])
                    for c in range(16):
                        ins = tensor.matmul(pO[:, 512 * h:512 * (h + 1)], wvt[:, 128 * c:128 * (c + 1)],
                                            wob[:, c, 512 * h:512 * (h + 1)], start=(c == 0), stop=(c == 15))
                    T.d(ins)

        @block.scalar
        def _(scalar):
            S = ESeq(scalar, sself)
            for item in SCHED:
                k = item[0]
                if k == "sr":
                    qi, kj = item[1], item[2]
                    scalar.wait_ge(tself, TM[("ac", qi, kj)])
                    S.d(scalar.copy(srow[:, qi % 2, 384 * kj:384 * (kj + 1)], pA[:, 0:384]))
                elif k == "fwt":
                    qi, kj = item[1], item[2]
                    gi = GPAIR[(qi, kj)]
                    s4 = SLOT[(qi, kj)]
                    scalar.wait_ge(vself, VM[("gb", qi, kj)])
                    if FC4[(qi, kj)] >= 2:
                        scalar.wait_ge(fr[s4], 16 * (FC4[(qi, kj)] - 1))
                    scalar.dma_start(out=flat[s4].rearrange("(r c) -> r c", c=513)[:, 0:512],
                                     in_=gbuf[:, gi % 2, :]).then_inc(fw[s4], 16)
                elif k == "expS":
                    q = item[1]
                    q2 = q % 2
                    L = row_geom(q)[0]
                    for s4 in range(4):
                        if CNT4END[q][s4] > 0:
                            scalar.wait_ge(fr[s4], 16 * CNT4END[q][s4])
                    scalar.wait_ge(gself, MSKLAST[q])
                    scalar.wait_ge(tself, TM[("ptr", q, G2(q)[2] - 1)])
                    scalar.wait_ge(vself, 3)
                    S.d(scalar.activation(out=srow[:, q2, 0:L], in_=srow[:, q2, 0:L], func=AF.Exp,
                                          bias=csts[:, 2:3], accum_out=ncol[:, q2, 2:3]))
                elif k == "pte":
                    q, g4 = item[1], item[2]
                    nkc = G2(q)[1]
                    nch = min(4, nkc - 4 * g4)
                    scalar.wait_ge(tself, TM[("ptr", q, g4)])
                    if g4 == 0 and q >= 1:
                        scalar.wait_ge(tself, TM[("okc", q - 1, G2(q - 1)[1] - 1)])
                    scalar.wait_ge(vself, 3)
                    S.d(scalar.activation(out=pts[:, 512 * g4:512 * g4 + 128 * nch],
                                          in_=pT[:, g4 % 2, 0:128 * nch], func=AF.Exp,
                                          bias=csts[:, 2:3]))
                elif k == "wve":
                    q, g4 = item[1], item[2]
                    scalar.wait_ge(tself, TM[("wvt", q, g4)])
                    if g4 == 0 and q >= 1:
                        scalar.wait_ge(tself, TM[("wo", q - 1, 1)])
                    base = 1024 + 512 * (g4 % 2)
                    S.d(scalar.copy(wvt[:, 512 * g4:512 * (g4 + 1)], pO[:, base:base + 512]))
                elif k == "fac":
                    q = item[1]
                    scalar.wait_ge(tself, TM[("wo", q, 1)])
                    if q >= 2:
                        scalar.wait_ge(vself, VM[("vln2", q - 2, 0)])
                        scalar.wait_ge(sr[q % 2], 16 * (q // 2))
                    S.d(scalar.copy(facc[:, q % 2, :], pO[:, 0:1024]))
                elif k == "sact":
                    q = item[1]
                    q2 = q % 2
                    scalar.wait_ge(vself, VM[("vln1", q, 0)])
                    scalar.wait_ge(vself, 3)
                    S.d(scalar.activation(out=tmpe[:], in_=facc[:, q2, :], func=AF.Square, accum_out=lnc[:, q2, 1:2]))
                    S.d(scalar.activation(out=lnc[:, q2, 3:4], in_=lnc[:, q2, 1:2], func=AF.Ln,
                                          scale=csts[:, 0:1], bias=csts[:, 1:2]))
                    S.d(scalar.activation(out=lnc[:, q2, 3:4], in_=lnc[:, q2, 3:4], func=AF.Exp, scale=-0.5))
    return nc


def build_stageC2():
    nc = bass.Bass("TRN2", target_bir_lowering=False)
    R2 = nc.dram_tensor("R2", [TT, D], F32, kind="ExternalInput").ap()
    QuT = nc.dram_tensor("QuT", [128, TT], BF16, kind="ExternalInput").ap()
    QvT = nc.dram_tensor("QvT", [128, TT], BF16, kind="ExternalInput").ap()
    KT = nc.dram_tensor("KT", [128, 4352], BF16, kind="ExternalInput").ap()
    Tbl = nc.dram_tensor("Tbl", [128, NW], BF16, kind="ExternalInput").ap()
    V_d = nc.dram_tensor("V_d", [4224, 2048], BF16, kind="ExternalInput").ap()
    G_d = nc.dram_tensor("G_d", [TT, 2048], BF16, kind="ExternalInput").ap()
    W_o = nc.dram_tensor("W_o", [2048, D], F32, kind="ExternalInput").ap()
    idn = nc.dram_tensor("idn", [128, 128], F32, kind="ExternalInput").ap()
    R3 = nc.dram_tensor("R3", [TT, D], F32, kind="ExternalOutput").ap()
    emit_stageC2(nc, R2, QuT, QvT, KT, Tbl, V_d, G_d, W_o, idn, R3)
    return nc



def emit_stageD(nc, R3, W_r2, W_k2, W_v2, mu_r, mu_k, idn, OUT, fuse=None):
    NTB = 8
    VPD = 7
    vx_m = lambda i: VPD + i + 1
    VXE = VPD + NT
    v_mix = lambda tb, cc, k: VXE + 92 * tb + 4 * cc + k
    v_rk = lambda tb, dc, k: VXE + 92 * tb + 32 + 4 * dc + k
    v_kv = lambda tb, dc: VXE + 92 * tb + 64 + dc + 1
    v_ep = lambda tb, t4, k: VXE + 92 * tb + 72 + 5 * t4 + k
    SWP = 48
    sx_m = lambda i: SWP + i + 1
    SXE = SWP + NT
    s_th = lambda tb, dc: SXE + 28 * tb + 2 * dc + 1
    s_sq = lambda tb, dc: SXE + 28 * tb + 2 * dc + 2
    s_ln = lambda tb, t4, k: SXE + 28 * tb + 16 + 3 * t4 + k
    TWP = 48
    tw_m = lambda m, cg: 8 * m + cg + 1
    tx_m = lambda i: TWP + i + 1
    TXE = TWP + NT
    t_r = lambda tb, dc: TXE + 28 * tb + 2 * dc + 1
    t_k = lambda tb, dc: TXE + 28 * tb + 2 * dc + 2
    t_kv = lambda tb, dc: TXE + 28 * tb + 16 + dc + 1
    t_f = lambda tb, t4: TXE + 28 * tb + 24 + t4 + 1
    with contextlib.ExitStack() as _stk:
        al = Alloc(nc, _stk, **(fuse or {}))
        idb = al.sb("g_idb", [128, 128], BF16)
        x3T = al.sb("g_x3T", [128, NC, TT + 1], BF16)
        wT = [al.sb("g_wrT", [128, NC, D], BF16),
              al.sb("g_wkT", [128, NC, D], BF16),
              al.sb("g_wvT", [128, NC, D], BF16)]
        wst = al.sb("g_wst", [128, 4, D], BF16)
        mur = al.sb("g_mur", [128, NC, 1], F32)
        muk = al.sb("g_muk", [128, NC, 1], F32)
        omr = al.sb("g_omr", [128, NC, 1], F32)
        omk = al.sb("g_omk", [128, NC, 1], F32)
        rf = al.sb("g_rf", [128, 2, D], F32)
        xb0 = al.sb("g_xb0", [128, 2, D], BF16)
        xrT = al.sb("g_xrT", [128, NC, 512], BF16)
        xkT = al.sb("g_xkT", [128, NC, 512], BF16)
        mixt = al.sb("g_mixt", [128, 512], F32)
        ghl = al.sb("g_ghl", [128, 2, 512], F32)
        thb = al.sb("g_thb", [128, 2, 512], F32)
        rb = al.sb("g_rb", [128, NC, 512], BF16)
        kab = al.sb("g_kab", [128, NC, 512], BF16)
        outT = al.sb("g_outT", [128, NC, 512], BF16)
        facc = al.sb("g_facc", [128, 2, D], F32)
        lnc = al.sb("g_lnc", [128, 2, 4], F32)
        tmpe = al.sb("g_tmpe", [128, D], F32)
        csts = al.sb("g_csts", [128, 2], F32)
        trip = al.sb("g_trip", [1, 1], F32)
        ptr = al.ps("g_ptr", [128, 2, D], BF16)
        pR = al.ps("g_pR", [128, 2, 512], F32)
        pF = al.ps("g_pF", [128, 2, D], BF16)
        aux = al.sem("g_aux")
        ld0 = al.sem("g_ld0"); ld1 = al.sem("g_ld1")
        lr0 = al.sem("g_lr0"); lr1 = al.sem("g_lr1")
        so0 = al.sem("g_so0"); so1 = al.sem("g_so1")
        vself = al.sem("g_vself"); sself = al.sem("g_sself"); tself = al.sem("g_tself")
        ld = [ld0, ld1]; lr = [lr0, lr1]; so = [so0, so1]
        register_const(nc, 1.0 / 1024.0, csts[:, 0:1])
        register_const(nc, 1e-5, csts[:, 1:2])
        block = al.block()
        WIN = [W_r2, W_k2, W_v2]

        P1 = [("wl", 0)]
        xti = 0
        for h in range(6):
            nx = 4 if h == 0 else 5
            for _ in range(nx):
                if xti < NT:
                    P1.append(("XT", xti)); xti += 1
            for cg in range(NC):
                P1.append(("WT", h, cg))
            if h + 1 < 6:
                P1.append(("wl", h + 1))
        while xti < NT:
            P1.append(("XT", xti)); xti += 1
        TXM, SXM, P1G = {}, {}, []
        tcnt = scnt = 0
        for it1 in P1:
            if it1[0] in ("WT", "XT"):
                tcnt += 1; TXM[it1] = tcnt
                scnt += 1; SXM[it1] = scnt
                P1G.append(it1)
        assert tcnt == TWP + NT

        SCHED2 = []
        for it2 in range(NTB + 1):
            mixq = []
            if it2 < NTB:
                for cc in range(NC):
                    for j in (0, 1):
                        mixq.append((it2, cc, j))
            if it2 == 0:
                for (tb2, cc, j) in mixq:
                    SCHED2 += [("mxs", tb2, cc, j), ("mxv", tb2, cc, j)]
                continue
            tb = it2 - 1
            SCHED2.append(("rmm", tb, 0))
            for dc in range(NC):
                if dc + 1 < NC:
                    SCHED2.append(("rmm", tb, dc + 1))
                SCHED2 += [("th", tb, dc), ("rb", tb, dc), ("kmm", tb, dc),
                           ("relu", tb, dc), ("kab", tb, dc)]
            for dc in range(NC):
                SCHED2 += [("vmm", tb, dc), ("ot", tb, dc)]
                for _ in range(2):
                    if mixq:
                        tb2, cc, j = mixq.pop(0)
                        SCHED2 += [("mxs", tb2, cc, j), ("mxv", tb2, cc, j)]
            for t4 in range(4):
                SCHED2 += [("pf", tb, t4), ("ep1", tb, t4), ("sln", tb, t4), ("ep2", tb, t4)]
            for (tb2, cc, j) in mixq:
                SCHED2 += [("mxs", tb2, cc, j), ("mxv", tb2, cc, j)]
        TM2, SM2, VM2 = {}, {}, {}
        t2n, s2n, v2n = TWP + NT, SWP + NT, VXE
        for item in SCHED2:
            k = item[0]
            if k in ("rmm", "kmm", "vmm", "pf"):
                t2n += 1; TM2[item] = t2n
            elif k in ("mxs", "th", "rb", "relu", "kab"):
                s2n += 1; SM2[item] = s2n
            elif k == "sln":
                s2n += 3; SM2[item] = s2n
            elif k in ("mxv", "ot", "ep2"):
                v2n += 1; VM2[item] = v2n
            elif k == "ep1":
                v2n += 4; VM2[item] = v2n

        @block.gpsimd
        def _(gp):
            gp.dma_start(out=idb[:], in_=idn).then_inc(aux, 16)
            gp.wait_ge(aux, 16)
            gp.dma_start(out=mur[:, :, 0], in_=mu_r.rearrange("(c p) -> p c", p=128), allow_slow_non_contiguous=True).then_inc(aux, 16)
            gp.wait_ge(aux, 32)
            gp.dma_start(out=muk[:, :, 0], in_=mu_k.rearrange("(c p) -> p c", p=128), allow_slow_non_contiguous=True).then_inc(aux, 16)
            gp.wait_ge(aux, 48)
            for it1 in P1:
                if it1[0] != "wl":
                    continue
                h = it1[1]
                m, hf = divmod(h, 2)
                if h >= 1:
                    gp.wait_ge(tself, TXM[("WT", h - 1, NC - 1)])
                gp.dma_start(out=wst[:],
                             in_=WIN[m].rearrange("(c p) o -> p c o", p=128)[:, 4 * hf:4 * (hf + 1), :]).then_inc(aux, 16)
                gp.wait_ge(aux, 16 * (4 + h))

        @block.sync
        def _(sync):
            for i in range(NT):
                if i >= 2:
                    sync.wait_ge(vself, vx_m(i - 2))
                sync.dma_start(out=rf[:, i % 2, :], in_=R3[128 * i:128 * (i + 1), :]).then_inc(ld[i % 2], 16)
            for g in range(4 * NTB):
                tb, t4 = divmod(g, 4)
                if g >= 2:
                    pg = g - 2
                    sync.wait_ge(vself, VM2[("ep1", pg // 4, pg % 4)] - 3)
                else:
                    sync.wait_ge(vself, vx_m(NT - 1))
                sync.dma_start(out=rf[:, g % 2, :],
                               in_=R3[128 * g:128 * (g + 1), :]).then_inc(lr[g % 2], 16)
                if g >= 1:
                    po = g - 1
                    sync.wait_ge(vself, VM2[("ep2", po // 4, po % 4)])
                    if po >= 2:
                        sync.wait_ge(so[po % 2], 16 * (po // 2))
                    sync.dma_start(out=OUT[128 * po:128 * (po + 1), :], in_=facc[:, po % 2, :]).then_inc(so[po % 2], 16)
            po = 4 * NTB - 1
            sync.wait_ge(vself, VM2[("ep2", po // 4, po % 4)])
            sync.wait_ge(so[po % 2], 16 * (po // 2))
            sync.dma_start(out=OUT[128 * po:128 * (po + 1), :], in_=facc[:, po % 2, :]).then_inc(so[po % 2], 16)
            sync.wait_ge(so0, 16 * 16)
            sync.wait_ge(so1, 16 * 16)

        @block.vector
        def _(vector):
            V = ESeq(vector, vself)
            V.d(vector.memset(csts[:, 0:1], 1.0 / 1024.0))
            V.d(vector.memset(csts[:, 1:2], 1e-5))
            V.d(vector.memset(trip[:], 0.0))
            V.d(vector.tensor_copy(trip[:].bitcast(I32), trip[:]))
            vector.wait_ge(aux, 48)
            V.d(vector.tensor_scalar(omr[:], mur[:], -1.0, 1.0, ALU.mult, ALU.add))
            V.d(vector.tensor_scalar(omk[:], muk[:], -1.0, 1.0, ALU.mult, ALU.add))
            V.d(vector.memset(x3T[:, :, 0:1], 0.0))
            assert V.n == VPD
            for i in range(NT):
                k2 = i % 2
                vector.wait_ge(ld[k2], 16 * (i // 2 + 1))
                if i >= 2:
                    vector.wait_ge(tself, TXM[("XT", i - 2)])
                V.d(vector.tensor_copy(xb0[:, k2, :], rf[:, k2, :]))
                assert V.n == vx_m(i)
            for item in SCHED2:
                k = item[0]
                if k == "mxv":
                    tb, cc, j = item[1], item[2], item[3]
                    if cc == 0 and j == 0:
                        vector.wait_ge(sself, sx_m(NT - 1))
                    if cc == 0 and tb >= 1:
                        vector.wait_ge(tself, TM2[("rmm" if j == 0 else "kmm", tb - 1, NC - 1)])
                    xs = x3T[:, cc, 1 + 512 * tb:1 + 512 * (tb + 1)]
                    vector.wait_ge(sself, SM2[("mxs", tb, cc, j)])
                    dst = xrT if j == 0 else xkT
                    mu = mur if j == 0 else muk
                    V.d(vector.scalar_tensor_tensor(dst[:, cc, :], xs, mu[:, cc, 0:1],
                                                    wst[:, 2 * (cc % 2) + j, 0:512], ALU.mult, ALU.add))
                elif k == "ot":
                    tb, dc = item[1], item[2]
                    k2 = dc % 2
                    if dc == 0 and tb >= 1:
                        vector.wait_ge(tself, TM2[("pf", tb - 1, 3)])
                    vector.wait_ge(tself, TM2[("vmm", tb, dc)])
                    vector.wait_ge(sself, SM2[("rb", tb, dc)])
                    V.d(vector.tensor_mul(outT[:, dc, :], pR[:, k2, :], rb[:, dc, :]))
                elif k == "ep1":
                    tb, t4 = item[1], item[2]
                    g = 4 * tb + t4
                    k2g = g % 2
                    vector.wait_ge(tself, TM2[("pf", tb, t4)])
                    vector.wait_ge(lr[k2g], 16 * (g // 2 + 1))
                    if g >= 2:
                        vector.wait_ge(so[k2g], 16 * (g // 2))
                    fa = facc[:, k2g, :]
                    V.d(vector.tensor_add(fa, pF[:, t4 % 2, :], rf[:, k2g, :]))
                    V.d(vector.tensor_reduce(lnc[:, k2g, 0:1], fa, AX.X, ALU.add, negate=True))
                    V.d(vector.tensor_scalar(lnc[:, k2g, 0:1], lnc[:, k2g, 0:1], 1.0 / D, None, ALU.mult))
                    V.d(vector.tensor_scalar(fa, fa, lnc[:, k2g, 0:1], None, ALU.add))
                elif k == "ep2":
                    tb, t4 = item[1], item[2]
                    k2g = (4 * tb + t4) % 2
                    vector.wait_ge(sself, SM2[("sln", tb, t4)])
                    V.d(vector.tensor_scalar(facc[:, k2g, :], facc[:, k2g, :], lnc[:, k2g, 3:4], None, ALU.mult))

        @block.tensor
        def _(tensor):
            T = ESeq(tensor, tself, relax=True)
            tensor.wait_ge(aux, 64)
            for gidx, it1 in enumerate(P1G):
                par = gidx % 2
                if gidx >= 2:
                    tensor.wait_ge(sself, SXM[P1G[gidx - 2]])
                if it1[0] == "WT":
                    h, cg = it1[1], it1[2]
                    hf = h % 2
                    tensor.wait_ge(aux, 16 * (4 + h))
                    for dh in range(4):
                        ins = tensor.transpose(ptr[:, par, 512 * hf + 128 * dh:512 * hf + 128 * (dh + 1)],
                                               wst[:, dh, 128 * cg:128 * (cg + 1)], idb[:])
                    T.d(ins)
                else:
                    i = it1[1]
                    k2 = i % 2
                    tensor.wait_ge(vself, vx_m(i))
                    for c in range(NC):
                        ins = tensor.transpose(ptr[:, par, 128 * c:128 * (c + 1)], xb0[:, k2, 128 * c:128 * (c + 1)], idb[:])
                    T.d(ins)
                assert T.n == TXM[it1]
            assert T.n == TWP + NT
            for item in SCHED2:
                k = item[0]
                if k == "rmm":
                    tb, dc = item[1], item[2]
                    k2 = dc % 2
                    if dc == 0:
                        tensor.wait_ge(vself, VM2[("mxv", tb, NC - 1, 0)])
                    if dc >= 2:
                        tensor.wait_ge(sself, SM2[("relu", tb, dc - 2)])
                    elif tb >= 1:
                        tensor.wait_ge(vself, VM2[("ot", tb - 1, 6 + dc)])
                    for cc in range(NC):
                        ins = tensor.matmul(pR[:, k2, :], wT[0][:, cc, 128 * dc:128 * (dc + 1)], xrT[:, cc, :],
                                            start=(cc == 0), stop=(cc == NC - 1))
                    T.d(ins)
                    assert T.n == TM2[item]
                elif k == "kmm":
                    tb, dc = item[1], item[2]
                    k2 = dc % 2
                    if dc == 0:
                        tensor.wait_ge(vself, VM2[("mxv", tb, NC - 1, 1)])
                    tensor.wait_ge(sself, SM2[("th", tb, dc)])
                    for cc in range(NC):
                        ins = tensor.matmul(pR[:, k2, :], wT[1][:, cc, 128 * dc:128 * (dc + 1)], xkT[:, cc, :],
                                            start=(cc == 0), stop=(cc == NC - 1))
                    T.d(ins)
                    assert T.n == TM2[item]
                elif k == "vmm":
                    tb, dc = item[1], item[2]
                    k2 = dc % 2
                    if dc == 0:
                        tensor.wait_ge(sself, SM2[("kab", tb, NC - 1)])
                    if dc >= 2:
                        tensor.wait_ge(vself, VM2[("ot", tb, dc - 2)])
                    else:
                        tensor.wait_ge(sself, SM2[("relu", tb, 6 + dc)])
                    for cc in range(NC):
                        ins = tensor.matmul(pR[:, k2, :], wT[2][:, cc, 128 * dc:128 * (dc + 1)], kab[:, cc, :],
                                            start=(cc == 0), stop=(cc == NC - 1))
                    T.d(ins)
                    assert T.n == TM2[item]
                elif k == "pf":
                    tb, t4 = item[1], item[2]
                    if t4 == 0:
                        tensor.wait_ge(vself, VM2[("ot", tb, NC - 1)])
                    g = 4 * tb + t4
                    if g >= 2:
                        p = g - 2
                        tensor.wait_ge(vself, VM2[("ep1", p // 4, p % 4)] - 3)
                    for dc in range(NC):
                        ins = tensor.transpose(pF[:, t4 % 2, 128 * dc:128 * (dc + 1)],
                                               outT[:, dc, 128 * t4:128 * (t4 + 1)], idb[:])
                    T.d(ins)
                    assert T.n == TM2[item]

        @block.scalar
        def _(scalar):
            S = ESeq(scalar, sself)
            for gidx, it1 in enumerate(P1G):
                par = gidx % 2
                scalar.wait_ge(tself, TXM[it1])
                if it1[0] == "WT":
                    h, cg = it1[1], it1[2]
                    m, hf = divmod(h, 2)
                    S.d(scalar.copy(wT[m][:, cg, 512 * hf:512 * (hf + 1)],
                                    ptr[:, par, 512 * hf:512 * (hf + 1)]))
                else:
                    i = it1[1]
                    S.d(scalar.copy(x3T[:, :, 1 + 128 * i:1 + 128 * (i + 1)],
                                    ptr[:, par, :].rearrange("p (c t) -> p c t", t=128)))
                assert S.n == SXM[it1]
            assert S.n == SWP + NT
            for item in SCHED2:
                k = item[0]
                if k == "mxs":
                    tb, cc, j = item[1], item[2], item[3]
                    if cc >= 2:
                        scalar.wait_ge(vself, VM2[("mxv", tb, cc - 2, j)])
                    xp = x3T[:, cc, 512 * tb:512 * (tb + 1)]
                    om = omr if j == 0 else omk
                    S.d(scalar.activation(out=wst[:, 2 * (cc % 2) + j, 0:512], in_=xp,
                                          func=AF.Copy, scale=om[:, cc, 0:1]))
                    assert S.n == SM2[item]
                elif k == "th":
                    tb, dc = item[1], item[2]
                    k2 = dc % 2
                    scalar.wait_ge(tself, TM2[("rmm", tb, dc)])
                    S.d(scalar.activation(out=thb[:, k2, :], in_=pR[:, k2, :], func=AF.Tanh, scale=0.5))
                    assert S.n == SM2[item]
                elif k == "rb":
                    tb, dc = item[1], item[2]
                    k2 = dc % 2
                    if tb >= 1:
                        scalar.wait_ge(vself, VM2[("ot", tb - 1, dc)])
                    S.d(scalar.activation(out=rb[:, dc, :], in_=thb[:, k2, :], func=AF.Copy,
                                          scale=0.5, bias=0.5))
                    assert S.n == SM2[item]
                elif k == "relu":
                    tb, dc = item[1], item[2]
                    k2 = dc % 2
                    scalar.wait_ge(tself, TM2[("kmm", tb, dc)])
                    S.d(scalar.activation(out=thb[:, k2, :], in_=pR[:, k2, :], func=AF.Relu))
                    assert S.n == SM2[item]
                elif k == "kab":
                    tb, dc = item[1], item[2]
                    k2 = dc % 2
                    if dc == 0 and tb >= 1:
                        scalar.wait_ge(tself, TM2[("vmm", tb - 1, NC - 1)])
                    S.d(scalar.activation(out=kab[:, dc, :], in_=thb[:, k2, :], func=AF.Square))
                    assert S.n == SM2[item]
                elif k == "sln":
                    tb, t4 = item[1], item[2]
                    g = 4 * tb + t4
                    k2g = g % 2
                    scalar.wait_ge(vself, VM2[("ep1", tb, t4)])
                    scalar.wait_ge(vself, 4)
                    S.d(scalar.activation(out=tmpe[:], in_=facc[:, k2g, :], func=AF.Square, accum_out=lnc[:, k2g, 1:2]))
                    S.d(scalar.activation(out=lnc[:, k2g, 3:4], in_=lnc[:, k2g, 1:2], func=AF.Ln,
                                          scale=csts[:, 0:1], bias=csts[:, 1:2]))
                    S.d(scalar.activation(out=lnc[:, k2g, 3:4], in_=lnc[:, k2g, 3:4], func=AF.Exp, scale=-0.5))
                    assert S.n == SM2[item]
    return nc


def build_stageD():
    nc = bass.Bass("TRN2", target_bir_lowering=False)
    R3 = nc.dram_tensor("R3", [TT, D], F32, kind="ExternalInput").ap()
    W_r2 = nc.dram_tensor("W_r2", [D, D], F32, kind="ExternalInput").ap()
    W_k2 = nc.dram_tensor("W_k2", [D, D], F32, kind="ExternalInput").ap()
    W_v2 = nc.dram_tensor("W_v2", [D, D], F32, kind="ExternalInput").ap()
    mu_r = nc.dram_tensor("mu_r", [D], F32, kind="ExternalInput").ap()
    mu_k = nc.dram_tensor("mu_k", [D], F32, kind="ExternalInput").ap()
    idn = nc.dram_tensor("idn", [128, 128], F32, kind="ExternalInput").ap()
    OUT = nc.dram_tensor("OUT", [TT, D], F32, kind="ExternalOutput").ap()
    emit_stageD(nc, R3, W_r2, W_k2, W_v2, mu_r, mu_k, idn, OUT)
    return nc



def build_fused(upto=9):
    nc = bass.Bass("TRN2", target_bir_lowering=False)
    EI = lambda n, sh, dt=F32: nc.dram_tensor(n, sh, dt, kind="ExternalInput").ap()
    x = EI("x", [TT, D])
    w1 = EI("w1", [D, D]); b1r = EI("b1r", [1, D])
    w23 = EI("w23", [D, 32]); b23r = EI("b23r", [1, 32])
    w_up = EI("w_up", [3072, D]); w_gate = EI("w_gate", [3072, D]); w_down = EI("w_down", [D, 3072])
    g_ln = EI("g_ln", [D]); W_q = EI("W_q", [D, 128]); W_k = EI("W_k", [D, 128])
    W_v = EI("W_v", [D, 2048]); W_g = EI("W_g", [D, 2048]); W_r = EI("W_r", [D, 128])
    xl_u = EI("xl_u", [128]); xl_v = EI("xl_v", [128]); W_o = EI("W_o", [2048, D])
    W_r2 = EI("W_r2", [D, D]); W_k2 = EI("W_k2", [D, D]); W_v2 = EI("W_v2", [D, D])
    mu_r = EI("mu_r", [D]); mu_k = EI("mu_k", [D])
    idn = EI("idn", [128, 128]); catT = EI("catT", [128, NC, NW], BF16)
    OUT = nc.dram_tensor("OUT", [TT, D], F32, kind="ExternalOutput").ap()
    IN_ = lambda n, sh, dt=F32: nc.dram_tensor(n, sh, dt, kind="Internal").ap()
    R1 = IN_("z_R1", [TT, D]); R2 = IN_("z_R2", [TT, D]); R3 = IN_("z_R3", [TT, D])
    rawu = IN_("z_rawu", [3072, TT]); rawg = IN_("z_rawg", [3072, TT])
    pbuf = IN_("z_p", [3072, TT]); opre = IN_("z_o", [3072, TT])
    QuT = IN_("z_QuT", [128, TT], BF16); QvT = IN_("z_QvT", [128, TT], BF16)
    KT = IN_("z_KT", [128, 4352], BF16); Tbl = IN_("z_Tbl", [128, NW], BF16)
    V_d = IN_("z_V", [4224, 2048], BF16); G_d = IN_("z_G", [TT, 2048], BF16)
    m1loc = IN_("z_m1l", [1, 8]); m1g = IN_("z_m1g", [1, 8])
    m2loc = IN_("z_m2l", [1, 16]); m2g_ = IN_("z_m2g", [1, 16])
    m3loc = IN_("z_m3l", [1, 8]); m3g = IN_("z_m3g", [1, 8])
    m4loc = IN_("z_m4l", [1, 16]); m4g = IN_("z_m4g", [1, 16])
    with contextlib.ExitStack() as gstk:
        bar = gstk.enter_context(nc.semaphore("z_bar"))
        pv = gstk.enter_context(nc.semaphore("z_pv"))
        csem = [gstk.enter_context(nc.semaphore("z_cs%d" % i)) for i in range(4)]
        gc = gstk.enter_context(nc.sbuf_tensor("z_gc", [128, 8], F32))
        register_const(nc, RC_MAGIC, gc[:, 0:1])
        register_const(nc, -RC_MAGIC, gc[:, 1:2])
        register_const(nc, 1.0 / 1024.0, gc[:, 2:3])
        register_const(nc, 1e-5, gc[:, 3:4])
        register_const(nc, 1e-6, gc[:, 4:5])
        base = [0]
        cnt = [0]

        def fuse():
            cnt[0] = 0
            return dict(semstack=gstk, bar=bar, bar_base=base[0], counter=cnt)

        with nc.Block() as _b0:
            bw0 = BlockWrap(_b0, bar, 0, [0])

            @bw0.vector
            def _(v):
                Vp = ESeq(v, pv)
                Vp.d(v.memset(gc[:, 0:1], RC_MAGIC))
                Vp.d(v.memset(gc[:, 1:2], -RC_MAGIC))
                Vp.d(v.memset(gc[:, 2:3], 1.0 / 1024.0))
                Vp.d(v.memset(gc[:, 3:4], 1e-5))
                Vp.d(v.memset(gc[:, 4:5], 1e-6))
        base[0] += 1

        def boundary(ci, cin, cout):
            with nc.Block() as _bb:
                bwb = BlockWrap(_bb, bar, base[0], [0])

                @bwb.gpsimd
                def _(gp):
                    gp.collective_compute("AllReduce", ALU.max, replica_groups=[list(range(8))],
                                          ins=[cin.opt()], outs=[cout.opt()]).then_inc(csem[ci])
                    gp.wait_ge(csem[ci], 1)
            base[0] += 1

        steps = [
            ("A", lambda: emit_stageA(nc, x, w1, b1r, w23, b23r, idn, R1, m1loc, fuse=fuse())),
            ("c0", lambda: boundary(0, m1loc, m1g)),
            ("B1", lambda: emit_stageB1(nc, R1, m1g, w_up, w_gate, idn, rawu, rawg,
                                        m2loc[0:1, 0:8], m2loc[0:1, 8:16], fuse=fuse())),
            ("c1", lambda: boundary(1, m2loc, m2g_)),
            ("B2", lambda: emit_stageB2(nc, rawg, m2g_[0:1, 8:16], pbuf, m3loc, fuse=fuse())),
            ("c2", lambda: boundary(2, m3loc, m3g)),
            ("B3", lambda: emit_stageB3(nc, pbuf, rawu, m2g_[0:1, 0:8], m3g, opre, m4loc, fuse=fuse())),
            ("c3", lambda: boundary(3, m4loc, m4g)),
            ("B4", lambda: emit_stageB4(nc, opre, m4g, R1, w_down, idn, R2, fuse=fuse())),
            ("C1", lambda: emit_stageC1(nc, R2, g_ln, W_q, W_k, W_v, W_g, W_r, xl_u, xl_v, idn, catT,
                                        QuT, QvT, KT, Tbl, V_d, G_d, fuse=fuse())),
            ("C2", lambda: emit_stageC2(nc, R2, QuT, QvT, KT, Tbl, V_d, G_d, W_o, idn, R3, fuse=fuse())),
            ("D", lambda: emit_stageD(nc, R3, W_r2, W_k2, W_v2, mu_r, mu_k, idn, OUT, fuse=fuse())),
        ]
        STAGE_STEPS = [0, 1, 3, 5, 7, 9, 10, 11, 12]
        nsteps = STAGE_STEPS[upto] if upto <= 8 else 12
        for nm, st in steps[:nsteps]:
            with nc.named_scope(nm):
                st()
            if cnt[0]:
                base[0] += cnt[0]
                cnt[0] = 0
    return nc


def kernel_fused(**inputs):
    f32 = np.float32
    x = np.ascontiguousarray(inputs["L0_attn_x"], dtype=f32)
    idn = np.eye(128, dtype=f32)
    catT = host_catT()
    w23 = np.ascontiguousarray(np.concatenate([inputs["L0_attn_W2"], inputs["L0_attn_W3"]], axis=1), dtype=f32)
    b23 = np.ascontiguousarray(np.concatenate([inputs["L0_attn_b2"], inputs["L0_attn_b3"]])[None, :], dtype=f32)
    cores = list(range(NCORE))
    nc = build_fused()
    common = {
        "w1": inputs["L0_attn_W1"], "b1r": np.ascontiguousarray(inputs["L0_attn_b1"][None, :]),
        "w23": w23, "b23r": b23,
        "w_up": inputs["L0_ffn_w_up"], "w_gate": inputs["L0_ffn_w_gate"], "w_down": inputs["L0_ffn_w_down"],
        "g_ln": inputs["L1_attn_g_ln"], "W_q": inputs["L1_attn_W_q"], "W_k": inputs["L1_attn_W_k"],
        "W_v": inputs["L1_attn_W_v"], "W_g": inputs["L1_attn_W_g"], "W_r": inputs["L1_attn_W_r"],
        "xl_u": inputs["L1_attn_xl_u"], "xl_v": inputs["L1_attn_xl_v"], "W_o": inputs["L1_attn_W_o"],
        "W_r2": inputs["L1_ffn_W_r"], "W_k2": inputs["L1_ffn_W_k"], "W_v2": inputs["L1_ffn_W_v"],
        "mu_r": inputs["L1_ffn_mu_r"], "mu_k": inputs["L1_ffn_mu_k"],
        "idn": idn, "catT": catT,
    }
    in_maps = [dict(common, x=np.ascontiguousarray(x[c])) for c in cores]
    res = run_bass_kernel_spmd(nc, in_maps, core_ids=cores)
    out = np.stack([np.asarray(res.results[c]["OUT"]) for c in cores])
    return out.astype(np.float32)


def kernel(**inputs):
    return kernel_fused(**inputs)


def kernel_unfused(**inputs):
    f32 = np.float32
    x = np.ascontiguousarray(inputs["L0_attn_x"], dtype=f32)
    idn = np.eye(128, dtype=f32)
    catT = host_catT()
    w23 = np.ascontiguousarray(np.concatenate([inputs["L0_attn_W2"], inputs["L0_attn_W3"]], axis=1), dtype=f32)
    b23 = np.ascontiguousarray(np.concatenate([inputs["L0_attn_b2"], inputs["L0_attn_b3"]])[None, :], dtype=f32)
    cores = list(range(NCORE))

    def row8(res, key):
        return np.array([[np.asarray(res.results[c][key]).reshape(-1)[0] for c in cores]], dtype=f32)

    nc = build_stageA()
    res = run_bass_kernel_spmd(nc, [{
        "x": np.ascontiguousarray(x[c]), "w1": inputs["L0_attn_W1"],
        "b1r": np.ascontiguousarray(inputs["L0_attn_b1"][None, :]),
        "w23": w23, "b23r": b23, "idn": idn} for c in cores], core_ids=cores)
    R1 = [np.asarray(res.results[c]["R1"]) for c in cores]
    m1s = row8(res, "m1")
    nc = build_stageB1()
    res = run_bass_kernel_spmd(nc, [{
        "R1": R1[c], "m1s": m1s, "w_up": inputs["L0_ffn_w_up"],
        "w_gate": inputs["L0_ffn_w_gate"], "idn": idn} for c in cores], core_ids=cores)
    raw_up = [np.asarray(res.results[c]["raw_up"]) for c in cores]
    raw_gate = [np.asarray(res.results[c]["raw_gate"]) for c in cores]
    m2u = row8(res, "m2u_o")
    m2g = row8(res, "m2g_o")
    nc = build_stageB2()
    res = run_bass_kernel_spmd(nc, [{
        "raw_gate": raw_gate[c], "m2g": m2g} for c in cores], core_ids=cores)
    pbuf = [np.asarray(res.results[c]["p_out"]) for c in cores]
    m3s = row8(res, "m3")
    nc = build_stageB3()
    res = run_bass_kernel_spmd(nc, [{
        "p_in": pbuf[c], "raw_up": raw_up[c], "m2u": m2u, "m3s": m3s} for c in cores], core_ids=cores)
    opre = [np.asarray(res.results[c]["o_out"]) for c in cores]
    m4s = row8(res, "m4")
    del pbuf, raw_up, raw_gate
    nc = build_stageB4()
    res = run_bass_kernel_spmd(nc, [{
        "o_pre": opre[c], "m4s": m4s, "R1": R1[c],
        "w_down": inputs["L0_ffn_w_down"], "idn": idn} for c in cores], core_ids=cores)
    R2 = [np.asarray(res.results[c]["R2"]) for c in cores]
    del opre, R1
    nc = build_stageC1()
    res = run_bass_kernel_spmd(nc, [{
        "R2": R2[c], "g_ln": inputs["L1_attn_g_ln"], "W_q": inputs["L1_attn_W_q"],
        "W_k": inputs["L1_attn_W_k"], "W_v": inputs["L1_attn_W_v"],
        "W_g": inputs["L1_attn_W_g"], "W_r": inputs["L1_attn_W_r"],
        "xl_u": inputs["L1_attn_xl_u"], "xl_v": inputs["L1_attn_xl_v"],
        "idn": idn, "catT": catT} for c in cores], core_ids=cores)
    c1 = [{k: np.asarray(res.results[c][k]) for k in ["QuT", "QvT", "KT", "Tbl", "V_d", "G_d"]} for c in cores]
    nc = build_stageC2()
    res = run_bass_kernel_spmd(nc, [{
        "R2": R2[c], "QuT": c1[c]["QuT"], "QvT": c1[c]["QvT"], "KT": c1[c]["KT"],
        "Tbl": c1[c]["Tbl"], "V_d": c1[c]["V_d"], "G_d": c1[c]["G_d"],
        "W_o": inputs["L1_attn_W_o"], "idn": idn} for c in cores], core_ids=cores)
    R3 = [np.asarray(res.results[c]["R3"]) for c in cores]
    del c1, R2
    nc = build_stageD()
    res = run_bass_kernel_spmd(nc, [{
        "R3": R3[c], "W_r2": inputs["L1_ffn_W_r"], "W_k2": inputs["L1_ffn_W_k"],
        "W_v2": inputs["L1_ffn_W_v"], "mu_r": inputs["L1_ffn_mu_r"],
        "mu_k": inputs["L1_ffn_mu_k"], "idn": idn} for c in cores], core_ids=cores)
    out = np.stack([np.asarray(res.results[c]["OUT"]) for c in cores])
    return out.astype(np.float32)
